# version 18
# baseline (speedup 1.0000x reference)
import contextlib
import numpy as np
import ml_dtypes
import concourse.bass as bass
import concourse.mybir as mybir
from concourse.bass_utils import run_bass_kernel_spmd
from concourse.aot_env import allow_ici_all_to_all

F32 = mybir.dt.float32
BF16 = mybir.dt.bfloat16
I32 = mybir.dt.int32
AF = mybir.ActivationFunctionType
OP = mybir.AluOpType
BFNP = ml_dtypes.bfloat16


def make_cfg(D=2048, L=4096, B=2, N=64, R=128, CHUNK=512, I=8192, NC=8):
    c = dict(D=D, L=L, B=B, N=N, R=R, CHUNK=CHUNK, I=I, NC=NC)
    c["GB"] = NC // B
    c["NCH"] = 4
    c["TC"] = L // c["NCH"]
    c["SP"] = c["TC"] // c["GB"]
    c["TCORE"] = c["NCH"] * c["SP"]
    assert c["TCORE"] == B * L // NC
    assert c["SP"] % 128 == 0
    c["DOWN"] = D // NC
    c["DT"] = c["DOWN"] // 128
    c["NT"] = c["DT"] * B
    c["KT"] = D // 128
    c["IOWN"] = I // NC
    c["IT"] = c["IOWN"] // 128
    c["HT"] = D // 128
    c["TS"] = min(512, c["TC"])
    c["NSUB"] = c["TC"] // c["TS"]
    c["CHW"] = min(CHUNK, c["TC"])
    c["REP"] = c["TC"] // c["CHW"]
    c["YC"] = min(512, c["TC"])
    c["YH"] = c["TC"] // c["YC"]
    c["NPROJ"] = R + 2 * N
    c["TPT"] = c["TCORE"] // 128
    c["NSEQ"] = min(512, c["TCORE"])
    c["NA"] = c["TCORE"] // c["NSEQ"]
    assert c["DT"] * c["YH"] * 2 <= 4 + 4
    assert c["DT"] >= 1 and c["IT"] >= 1
    return c


CFG = make_cfg()


class Sem:
    __slots__ = ("h", "v")

    def __init__(self, handle):
        self.h = handle
        self.v = 0

    def at(self):
        return [(self.h, self.v)]


class Sched:
    def __init__(self):
        self.items = []

    def do(self, eng, fn, wait=(), inc=None, k=1):
        byid = {}
        for (h, v) in wait:
            if v > 0 and (id(h) not in byid or byid[id(h)][1] < v):
                byid[id(h)] = (h, v)
        waits = list(byid.values())
        incs = None
        if inc is not None:
            inc.v += k
            incs = (inc.h, k)
        self.items.append((eng, fn, waits, incs))

    def emit(self, nc):
        with nc.Block() as block:
            for name in ("sync", "scalar", "vector", "tensor", "gpsimd"):
                items = [it for it in self.items if it[0] == name]
                if not items:
                    continue

                def body(eng, items=items):
                    hi = {}
                    for _, fn, waits, incs in items:
                        for (h, v) in waits:
                            if hi.get(id(h), 0) < v:
                                eng.wait_ge(h, v)
                                hi[id(h)] = v
                        inst = fn(eng) if fn is not None else None
                        if incs is not None:
                            assert inst is not None
                            inst.then_inc(incs[0], incs[1])

                getattr(block, name)(body)


def build_program(cfg):
    D, L, B, N, R = cfg["D"], cfg["L"], cfg["B"], cfg["N"], cfg["R"]
    I, NC = cfg["I"], cfg["NC"]
    GB, NCH, TC, SP, TCORE = cfg["GB"], cfg["NCH"], cfg["TC"], cfg["SP"], cfg["TCORE"]
    DOWN, DT, NT, KT = cfg["DOWN"], cfg["DT"], cfg["NT"], cfg["KT"]
    IOWN, IT, HT, TS, NSUB = cfg["IOWN"], cfg["IT"], cfg["HT"], cfg["TS"], cfg["NSUB"]
    REP, CHW, YC, YH, NPROJ = (cfg["REP"], cfg["CHW"], cfg["YC"], cfg["YH"],
                               cfg["NPROJ"])
    TPT, NSEQ, NA = cfg["TPT"], cfg["NSEQ"], cfg["NA"]

    nc = bass.Bass("TRN2", target_bir_lowering=False)
    es = nc.ctx

    def din(name, shape, dtype=F32):
        return nc.dram_tensor(name, list(shape), dtype, kind="ExternalInput")

    x_own = din("x_own", [TCORE, D])
    wgT = din("wgT", [D, IOWN], I32)
    wuT = din("wuT", [D, IOWN], I32)
    wdT = din("wdT", [IOWN, D], I32)
    gs_in = din("gs", [128, IT])
    us_in = din("us", [128, IT])
    ds_in = din("ds", [1, D])
    opT = din("opT", [DOWN, D])
    xpT = din("xpT", [D, NPROJ])
    dpT = din("dpT", [R, D])
    dpb_in = din("dpb", [128, KT])
    nw_in = din("nw", [128, KT])
    nb_in = din("nb", [128, KT])
    dpar_in = din("dpar", [128, DT])
    alog_in = din("alog", [128, DT * N])
    id_bf_in = din("ident_bf", [128, 128], BF16)
    id_f_in = din("ident_f32", [128, 128], F32)
    fc_in = din("fconst", [128, 8], F32)

    y_out = nc.dram_tensor("y_out", [NCH, SP, D], F32, kind="ExternalOutput")

    xn_ag_in = nc.dram_tensor("xn_ag_in", [D, TCORE], BF16)
    xn_ag_out = nc.dram_tensor("xn_ag_out", [NC * D, TCORE], BF16)
    bc_ag_in = nc.dram_tensor("bc_ag_in", [2 * N, TCORE], BF16)
    bc_ag_out = nc.dram_tensor("bc_ag_out", [NC * 2 * N, TCORE], BF16)
    dt_a2a_in = nc.dram_tensor("dt_a2a_in", [D, TCORE], BF16)
    dt_a2a_out = nc.dram_tensor("dt_a2a_out", [D, TCORE], BF16)
    dx_a2a_in = nc.dram_tensor("dx_a2a_in", [D, TCORE], BF16)
    dx_a2a_out = nc.dram_tensor("dx_a2a_out", [D, TCORE], BF16)
    xn_a2a_out = nc.dram_tensor("xn_a2a_out", [D, TCORE], BF16)
    rs_in = [nc.dram_tensor(f"rs_in_{c}", [NC * D, SP], F32) for c in range(NCH)]
    rs_out = [nc.dram_tensor(f"rs_out_{c}", [D, SP], F32) for c in range(NCH)]

    def sb(name, shape, dtype):
        return es.enter_context(nc.sbuf_tensor(name, list(shape), dtype))

    a_t = sb("a_t", [128, DT * N], F32)
    a_scr = sb("a_scr", [128, DT * N], F32)
    gs_t = sb("gs_t", [128, IT], F32)
    us_t = sb("us_t", [128, IT], F32)
    opT_t = [sb(f"opT{i}", [128, D], BF16) for i in range(DT)]
    dpb_t = sb("dpb_t", [128, KT], F32)
    nw_t = sb("nw_t", [128, KT], F32)
    nb_t = sb("nb_t", [128, KT], F32)
    dpar_t = sb("dpar_t", [128, DT], F32)
    id_bf = sb("id_bf", [128, 128], BF16)
    id_f = sb("id_f", [128, 128], F32)
    fc_t = sb("fc_t", [128, 8], F32)
    FCONST = [1e-5, -0.5, -1.0, 1.0, -128.0, 0.0, 2.0, 0.5]
    for fi, fv in enumerate(FCONST):
        nc.const_aps.register(F32, fv, fc_t[:, fi:fi + 1])

    pb = [es.enter_context(nc.psum_tensor(f"pb{i}", [128, 512], F32))
          for i in range(8)]
    pb_bf = [p[:, :].bitcast(BF16) for p in pb]

    S = Sched()
    _sems = {}

    def sem(name):
        if name not in _sems:
            _sems[name] = Sem(es.enter_context(nc.semaphore(name)))
        return _sems[name]

    FREE = {}
    RDY = {}

    def fget(nm):
        return FREE.get(nm, [])

    def rget(nm):
        r = RDY.get(nm)
        assert r is not None, f"missing RDY[{nm}]"
        return r

    s_cd = sem("s_cd")
    s_cc0 = sem("s_cc0")
    for (dst, src) in [(gs_t, gs_in), (us_t, us_in), (dpb_t, dpb_in),
                       (nw_t, nw_in), (nb_t, nb_in), (dpar_t, dpar_in),
                       (id_bf, id_bf_in), (id_f, id_f_in), (a_scr, alog_in),
                       (fc_t, fc_in)]:
        S.do("sync", lambda e, d=dst, s=src: e.dma_start(out=d[:, :], in_=s[:, :]),
             inc=s_cd, k=16)
    c_cd = s_cd.at()
    S.do("scalar", lambda e: e.activation(a_t[:, :], a_scr[:, :], AF.Exp),
         wait=c_cd, inc=s_cc0)
    S.do("vector", lambda e: e.tensor_scalar_mul(a_scr[:, :], a_t[:, :], -1.0),
         wait=s_cc0.at(), inc=s_cc0)
    RDY["a_t"] = s_cc0.at()
    RDY["consts"] = c_cd

    esA = es.enter_context(contextlib.ExitStack())

    def sbA(name, shape, dtype):
        return esA.enter_context(nc.sbuf_tensor(name, list(shape), dtype))

    wsm = sbA("wsm", [128, D], F32)
    xpT_t = sbA("xpT_t", [128, KT * NPROJ], BF16)
    dpT_t = sbA("dpT_t", [128, D], BF16)
    xa_xb = [sbA(f"xa_x{p}", [128, D], F32) for p in range(2)]
    xa_st = sbA("xa_st", [128, D], BF16)
    xa_st2 = sbA("xa_st2", [128, D], BF16)
    xa_tr = sbA("xa_tr", [128, 1024], BF16)
    NST = max(1, D // 512)
    xa_stat = sbA("xa_stat", [128, 6 * NST + 6], F32)
    xa_dr = sbA("xa_dr", [128, TCORE], BF16)
    xa_bc = sbA("xa_bc", [128, TCORE], BF16)
    xa_xn2 = [sbA(f"xa_xn2_{k}", [128, TCORE], BF16) for k in range(KT)]
    xa_dtT = [sbA(f"xa_dtT_{k}", [128, TCORE], BF16) for k in range(KT)]
    xa_sp = [sbA(f"xa_sp{p}", [128, NSEQ], F32) for p in range(2)]

    s_pref = sem("s_pref")
    for c in range(NCH):
        S.do("sync", lambda e, c=c: e.dma_start(
            out=y_out[c, :, :], in_=x_own[c * SP:(c + 1) * SP, :]),
            inc=s_pref, k=16)
    RDY["prefill"] = s_pref.at()

    s_sw = sem("s_sw")
    s_swc = sem("s_swc")
    for i in range(DT):
        S.do("sync", lambda e, i=i: e.dma_start(
            out=wsm[:, 0:D], in_=opT[i * 128:(i + 1) * 128, :]),
            wait=fget("wsm"), inc=s_sw, k=16)
        S.do("scalar", lambda e, i=i: e.copy(opT_t[i][:, :], wsm[:, 0:D]),
             wait=s_sw.at(), inc=s_swc)
        FREE["wsm"] = s_swc.at()
    for k in range(KT):
        S.do("sync", lambda e, k=k: e.dma_start(
            out=wsm[:, 0:NPROJ], in_=xpT[k * 128:(k + 1) * 128, :]),
            wait=fget("wsm"), inc=s_sw, k=16)
        S.do("scalar", lambda e, k=k: e.copy(
            xpT_t[:, k * NPROJ:(k + 1) * NPROJ], wsm[:, 0:NPROJ]),
            wait=s_sw.at(), inc=s_swc)
        FREE["wsm"] = s_swc.at()
    S.do("sync", lambda e: e.dma_start(out=wsm[:, 0:D], in_=dpT[:, :]),
         wait=fget("wsm"), inc=s_sw, k=16)
    S.do("scalar", lambda e: e.copy(dpT_t[:, :], wsm[:, 0:D]),
         wait=s_sw.at(), inc=s_swc)
    RDY["smallw"] = s_swc.at()

    s_xd = sem("s_xd")
    s_ln = sem("s_ln")
    s_tp = sem("s_tp")
    s_tpe = sem("s_tpe")
    s_agw = sem("s_agw")
    s_cc = sem("s_cc")
    TRT = max(1, D // 1024)

    for ti in range(TPT):
        pX = ti % 2
        S.do("sync", lambda e, ti=ti, pX=pX: e.dma_start(
            out=xa_xb[pX][:, :], in_=x_own[ti * 128:(ti + 1) * 128, :]),
            wait=fget(f"xax{pX}"), inc=s_xd, k=16)
        cx = s_xd.at()

        def ln_pass(src_ap, dst, wait0, wfree):
            st6 = xa_stat[:, 0:6 * NST]
            for j in range(NST):
                w = min(512, D)
                S.do("vector", lambda e, j=j, s=src_ap, w=w: e.bn_stats(
                    st6[:, 6 * j:6 * (j + 1)], s[:, w * j:w * (j + 1)]),
                    wait=wait0, inc=s_ln)
            mv = xa_stat[:, 6 * NST:6 * NST + 2]
            S.do("vector", lambda e, mv=mv: e.bn_aggr(mv, st6),
                 wait=s_ln.at(), inc=s_ln)
            lnv = xa_stat[:, 6 * NST + 4:6 * NST + 5]
            rstd = xa_stat[:, 6 * NST + 2:6 * NST + 3]
            nmr = xa_stat[:, 6 * NST + 3:6 * NST + 4]
            S.do("scalar", lambda e, mv=mv, lv=lnv: e.activation(
                lv, mv[:, 1:2], AF.Ln, bias=1e-5),
                wait=s_ln.at() + RDY["consts"], inc=s_ln)
            S.do("scalar", lambda e, lv=lnv, r=rstd: e.activation(
                r, lv, AF.Exp, scale=-0.5), wait=s_ln.at(), inc=s_ln)
            S.do("vector", lambda e, mv=mv, r=rstd, nm=nmr:
                 e.scalar_tensor_tensor(nm, mv[:, 0:1], -1.0, r, OP.mult,
                                        OP.mult),
                 wait=s_ln.at(), inc=s_ln)
            S.do("scalar", lambda e, s=src_ap, d=dst, r=rstd, nm=nmr:
                 e.activation(d[:, :], s[:, :], AF.Identity, scale=r, bias=nm),
                 wait=s_ln.at() + wfree, inc=s_ln)
            return s_ln.at()

        c_xn = ln_pass(xa_xb[pX], xa_st, cx, fget("xa_st"))
        FREE[f"xax{pX}"] = c_xn
        c_xnn = ln_pass(xa_st, xa_st2, c_xn, fget("xa_st2"))

        for (src, kind, cready) in ((xa_st, "xn", c_xn), (xa_st2, "xnn", c_xnn)):
            for half in range(TRT):
                bk = 4 + (half % 2)
                bank = pb_bf[bk]
                nj = min(8, D // 128)
                for j in range(nj):
                    kk = half * 8 + j
                    S.do("tensor", lambda e, s=src, kk=kk, j=j, bank=bank:
                         e.transpose(bank[:, j * 128:(j + 1) * 128],
                                     s[:, kk * 128:(kk + 1) * 128], id_bf[:, :]),
                         wait=cready + RDY["consts"] + fget(f"pbank{bk}"),
                         inc=s_tp)
                c_t = s_tp.at()
                if kind == "xn":
                    S.do("scalar", lambda e, bank=bank, nj=nj: e.copy(
                        xa_tr[:, 0:nj * 128], bank[:, 0:nj * 128]),
                        wait=c_t + fget("xa_tr"), inc=s_tpe)
                    FREE[f"pbank{bk}"] = s_tpe.at()
                    S.do("sync", lambda e, ti=ti, half=half, nj=nj: e.dma_start(
                        out=xn_ag_in.ap().rearrange("(a p) t -> p a t", p=128)
                        [:, half * 8:half * 8 + nj, ti * 128:(ti + 1) * 128],
                        in_=xa_tr[:, 0:nj * 128].rearrange(
                            "p (a t) -> p a t", a=nj)),
                        wait=s_tpe.at(), inc=s_agw, k=16)
                    FREE["xa_tr"] = s_agw.at()
                else:
                    for j in range(nj):
                        kk = half * 8 + j
                        S.do("scalar", lambda e, kk=kk, j=j, bank=bank, ti=ti:
                             e.activation(
                                 xa_xn2[kk][:, ti * 128:(ti + 1) * 128],
                                 bank[:, j * 128:(j + 1) * 128], AF.Identity,
                                 scale=nw_t[:, kk:kk + 1],
                                 bias=nb_t[:, kk:kk + 1]),
                             wait=c_t + RDY["consts"], inc=s_tpe)
                    FREE[f"pbank{bk}"] = s_tpe.at()
            if kind == "xn":
                FREE["xa_st"] = s_tp.at() + c_xnn
            else:
                FREE["xa_st2"] = s_tp.at()
    c_xn2 = s_tpe.at()

    s_ccxn = sem("s_ccxn")
    s_ccxa = sem("s_ccxa")
    S.do("gpsimd", lambda e: e.collective_compute(
        "AllGather", OP.bypass, replica_groups=[list(range(NC))],
        ins=[xn_ag_in.ap().opt()], outs=[xn_ag_out.ap().opt()]),
        wait=s_agw.at(), inc=s_ccxn, k=1)
    RDY["xn_ag"] = s_ccxn.at()
    S.do("gpsimd", lambda e: e.collective_compute(
        "AllToAll", OP.bypass, replica_groups=[list(range(NC))],
        ins=[xn_ag_in.ap().opt()], outs=[xn_a2a_out.ap().opt()]),
        wait=s_agw.at(), inc=s_ccxa, k=1)
    RDY["xn_a2a"] = s_ccxa.at()

    s_mm = sem("s_mm")
    s_ev = sem("s_ev")
    s_dw = sem("s_dw")
    for na in range(NA):
        for m in range(NPROJ // 128):
            bank = pb[m]
            for k in range(KT):
                w0 = (c_xn2 + RDY["smallw"] + fget(f"pbank{m}")) if k == 0 else []
                S.do("tensor", lambda e, na=na, m=m, k=k, bank=bank: e.matmul(
                    bank[:, 0:NSEQ],
                    xpT_t[:, k * NPROJ + m * 128:k * NPROJ + (m + 1) * 128],
                    xa_xn2[k][:, na * NSEQ:(na + 1) * NSEQ],
                    start=(k == 0), stop=(k == KT - 1)),
                    wait=w0, inc=s_mm)
        S.do("scalar", lambda e, na=na: e.copy(
            xa_dr[:, na * NSEQ:(na + 1) * NSEQ], pb[0][:, 0:NSEQ]),
            wait=s_mm.at(), inc=s_ev)
        S.do("scalar", lambda e, na=na: e.copy(
            xa_bc[:, na * NSEQ:(na + 1) * NSEQ], pb[1][:, 0:NSEQ]),
            wait=s_mm.at(), inc=s_ev)
        FREE["pbank0"] = s_ev.at()
        FREE["pbank1"] = s_ev.at()
    c_proj = s_ev.at()
    s_bcw = sem("s_bcw")
    s_ccbc = sem("s_ccbc")
    S.do("sync", lambda e: e.dma_start(out=bc_ag_in.ap(), in_=xa_bc[0:2 * N, :]),
         wait=c_proj, inc=s_bcw, k=16)
    S.do("gpsimd", lambda e: e.collective_compute(
        "AllGather", OP.bypass, replica_groups=[list(range(NC))],
        ins=[bc_ag_in.ap().opt()], outs=[bc_ag_out.ap().opt()]),
        wait=s_bcw.at(), inc=s_ccbc, k=1)
    RDY["bc_ag"] = s_ccbc.at()

    s_dx = sem("s_dx")
    zb, rl = xa_sp[0], xa_sp[1]
    a1 = wsm[:, 0:NSEQ]
    a2 = wsm[:, NSEQ:2 * NSEQ]
    for m in range(KT):
        for na in range(NA):
            bk = 4 + ((m * NA + na) % 2)
            bank = pb[bk]
            S.do("tensor", lambda e, m=m, na=na, bank=bank: e.matmul(
                bank[:, 0:NSEQ], dpT_t[:, m * 128:(m + 1) * 128],
                xa_dr[:, na * NSEQ:(na + 1) * NSEQ], start=True, stop=True),
                wait=c_proj + RDY["smallw"] + fget(f"pbank{bk}"), inc=s_mm)
            S.do("scalar", lambda e, m=m, bank=bank: e.activation(
                zb[:, :], bank[:, 0:NSEQ], AF.Identity,
                bias=dpb_t[:, m:m + 1]),
                wait=s_mm.at() + fget("sp"), inc=s_ev)
            FREE[f"pbank{bk}"] = s_ev.at()
            S.do("scalar", lambda e: e.activation(a1[:, :], zb[:, :], AF.Abs),
                 wait=s_ev.at() + fget("wsm"), inc=s_ev)
            S.do("scalar", lambda e: e.activation(
                a2[:, :], a1[:, :], AF.Exp, scale=-1.0),
                wait=s_ev.at() + RDY["consts"], inc=s_ev)
            S.do("scalar", lambda e: e.activation(
                a1[:, :], a2[:, :], AF.Ln, bias=1.0), wait=s_ev.at(), inc=s_ev)
            S.do("scalar", lambda e: e.activation(rl[:, :], zb[:, :], AF.Relu),
                 wait=s_ev.at(), inc=s_ev)
            S.do("vector", lambda e, m=m, na=na: e.tensor_add(
                xa_dtT[m][:, na * NSEQ:(na + 1) * NSEQ], rl[:, :], a1[:, :]),
                wait=s_ev.at(), inc=s_ev)
            FREE["sp"] = s_ev.at()
            FREE["wsm"] = s_ev.at()
    c_dt = s_ev.at()
    s_dtw = sem("s_dtw")
    s_dxw = sem("s_dxw")
    s_ccdt = sem("s_ccdt")
    s_ccdx = sem("s_ccdx")
    for m in range(KT):
        S.do("sync", lambda e, m=m: e.dma_start(
            out=dt_a2a_in[m * 128:(m + 1) * 128, :], in_=xa_dtT[m][:, :]),
            wait=c_dt, inc=s_dtw, k=16)
        S.do("vector", lambda e, m=m: e.tensor_mul(
            xa_dr[:, :], xa_dtT[m][:, :], xa_xn2[m][:, :]),
            wait=c_dt + fget("xa_dr"), inc=s_dx)
        S.do("sync", lambda e, m=m: e.dma_start(
            out=dx_a2a_in[m * 128:(m + 1) * 128, :], in_=xa_dr[:, :]),
            wait=s_dx.at(), inc=s_dxw, k=16)
        FREE["xa_dr"] = s_dxw.at()
    c_a2aw = s_dtw.at() + s_dxw.at()
    S.do("gpsimd", lambda e: e.collective_compute(
        "AllToAll", OP.bypass, replica_groups=[list(range(NC))],
        ins=[dt_a2a_in.ap().opt()], outs=[dt_a2a_out.ap().opt()]),
        wait=c_a2aw, inc=s_ccdt, k=1)
    RDY["dt_a2a"] = s_ccdt.at()
    S.do("gpsimd", lambda e: e.collective_compute(
        "AllToAll", OP.bypass, replica_groups=[list(range(NC))],
        ins=[dx_a2a_in.ap().opt()], outs=[dx_a2a_out.ap().opt()]),
        wait=c_a2aw, inc=s_ccdx, k=1)
    RDY["dx_a2a"] = s_ccdx.at()

    esA.close()

    dsb = sb("dsb", [128, D], BF16)
    dt_ch = [sb(f"dt_ch{i}", [128, TC], BF16) for i in range(NT)]
    dx_ch = [sb(f"dx_ch{i}", [128, TC], BF16) for i in range(NT)]
    NDA = 5
    da_b = [sb(f"da{i}", [128, TC], BF16) for i in range(NDA)]
    u_b = [sb("u0", [128, TC], BF16)] * 2
    h_b = [sb(f"h{i}", [128, TC], BF16) for i in range(2)]
    z_b = [sb(f"z{i}", [128, TC], BF16) for i in range(2)]
    NBB = 4
    bb_b = [sb(f"bb{i}", [128, TC], BF16) for i in range(NBB)]
    xnd_t = [sb(f"xnd{i}", [128, TC], BF16) for i in range(2)]
    cb_b = [sb(f"cb{i}", [128, CHW], BF16) for i in range(NBB)]
    carry = [sb(f"carry{i}", [128, N], BF16) for i in range(NT)]
    rhs_t = [sb(f"rhs{i}", [128, TC], BF16) for i in range(NT)]
    xns = [sb(f"xns_{k}", [128, TS], BF16) for k in range(KT)]
    hsub_a = sb("hsub_a", [128, IT * TS], BF16)
    hsub = [hsub_a[:, i * TS:(i + 1) * TS] for i in range(IT)]
    sg_b = [sb(f"sg{p}", [128, TS], BF16) for p in range(2)]
    usb_b = [sb(f"usb{p}", [128, TS], BF16) for p in range(2)]
    ev_b = [sb(f"ev{i}", [128, 512], F32) for i in range(2)]

    warena = sb("warena", [128, 2 * KT * IOWN + IT * D], BF16)
    wg_t = [warena[:, k * IOWN:(k + 1) * IOWN] for k in range(KT)]
    wu_t = [warena[:, (KT + k) * IOWN:(KT + k + 1) * IOWN] for k in range(KT)]
    wd_t = [warena[:, 2 * KT * IOWN + k * D:2 * KT * IOWN + (k + 1) * D]
            for k in range(IT)]
    WSW = min(512, max(IOWN, D))
    wstage = [sb(f"wstage{p}", [128, WSW], I32) for p in range(2)]

    s_dsb = sem("s_dsb")
    S.do("gpsimd", lambda e: e.dma_start(
        out=dsb[:, :], in_=ds_in.ap().partition_broadcast(128)),
        wait=c_a2aw, inc=s_dsb, k=16)
    RDY["dsb"] = s_dsb.at()

    s_wd = [sem("s_wd0"), sem("s_wd1")]
    s_wc = sem("s_wc")
    s_wm = sem("s_wm")
    n_w = 0
    for (src, tiles, nt_, is_down, width) in (
            (wgT, wg_t, KT, False, IOWN), (wuT, wu_t, KT, False, IOWN),
            (wdT, wd_t, IT, True, D)):
        for k in range(nt_):
            for h0 in range(0, width, WSW):
                w = min(WSW, width - h0)
                p = n_w % 2
                w0 = fget(f"wstage{p}")
                if n_w == 0:
                    w0 = w0 + c_a2aw + s_dx.at()
                S.do("sync", lambda e, src=src, k=k, p=p, h0=h0, w=w:
                     e.dma_start(out=wstage[p][:, 0:w],
                                 in_=src[k * 128:(k + 1) * 128, h0:h0 + w]),
                     wait=w0, inc=s_wd[p], k=16)
                if not is_down:
                    S.do("scalar", lambda e, tiles=tiles, k=k, p=p, h0=h0, w=w:
                         e.activation(tiles[k][:, h0:h0 + w],
                                      wstage[p][:, 0:w], AF.Copy, bias=-128.0),
                         wait=s_wd[p].at() + RDY["consts"], inc=s_wc)
                    FREE[f"wstage{p}"] = s_wc.at()
                else:
                    for t0 in range(0, w, TS):
                        tw = min(TS, w - t0)
                        pp = (h0 // WSW + t0 // TS) % 2
                        S.do("scalar", lambda e, p=p, t0=t0, tw=tw, pp=pp:
                             e.activation(usb_b[pp][:, 0:tw],
                                          wstage[p][:, t0:t0 + tw], AF.Copy,
                                          bias=-128.0),
                             wait=s_wd[p].at() + fget(f"usb{pp}") +
                             RDY["consts"], inc=s_wc)
                        S.do("vector", lambda e, tiles=tiles, k=k, h0=h0,
                             t0=t0, tw=tw, pp=pp: e.tensor_mul(
                                 tiles[k][:, h0 + t0:h0 + t0 + tw],
                                 usb_b[pp][:, 0:tw],
                                 dsb[:, h0 + t0:h0 + t0 + tw]),
                             wait=s_wc.at() + RDY["dsb"], inc=s_wm)
                        FREE[f"usb{pp}"] = s_wm.at()
                    FREE[f"wstage{p}"] = s_wc.at()
                n_w += 1
    RDY["weights"] = s_wc.at() + s_wm.at()

    s_sc = sem("s_sc")
    nper = CHW // SP
    bc_v = bc_ag_out.ap().rearrange("(b s r) t -> b r s t", b=B, s=GB)

    s_bb = sem("s_bb")
    s_da = sem("s_da")
    s_u = sem("s_u")
    s_h = sem("s_h")
    s_z = sem("s_z")
    s_ymm = sem("s_ymm")
    s_yev = sem("s_yev")
    s_rhs = sem("s_rhs")
    s_cp = sem("s_cp")
    s_xns = sem("s_xns")
    s_gu = sem("s_gu")
    s_sil = sem("s_sil")
    s_hs = sem("s_hs")
    s_dn = sem("s_dn")
    s_oev = sem("s_oev")
    s_orw = sem("s_orw")
    s_rs = sem("s_rs")
    s_fd = sem("s_fd")
    s_ftp = sem("s_ftp")
    s_fa = sem("s_fa")
    s_out = sem("s_out")

    it_scan = [0]
    pend = []

    def pop_pend(k):
        for _ in range(min(k, len(pend))):
            pend.pop(0)()

    def emit_chunk_loads(c, only_b=None):
        for i in range(NT):
            b, half = i // DT, i % DT
            if only_b is not None and b != only_b:
                continue

            def mk(dst, srcd, b=b, half=half, c=c):
                def f(e):
                    a = srcd.ap().rearrange("(j dd) t -> j dd t", j=NC)
                    a = a.rearrange("(b s) dd t -> b s dd t", b=B)
                    a = a[b, :, half * 128:(half + 1) * 128,
                          c * SP:(c + 1) * SP]
                    a = a.transpose([1, 0, 2])
                    return e.dma_start(
                        out=dst[:, :].rearrange("p (s t) -> p s t", s=GB),
                        in_=a)
                return f

            sdt = sem(f"s_dtch{i}")
            sdx = sem(f"s_dxch{i}")
            S.do("sync", mk(dt_ch[i], dt_a2a_out),
                 wait=RDY["dt_a2a"] + fget(f"dt_ch{i}"), inc=sdt, k=16)
            RDY[f"dt_ch{i}"] = sdt.at()
            S.do("sync", mk(dx_ch[i], dx_a2a_out),
                 wait=RDY["dx_a2a"] + fget(f"dx_ch{i}"), inc=sdx, k=16)
            RDY[f"dx_ch{i}"] = sdx.at()


    def queue_ffn_sub(c, b, hf):
        for k in range(KT):
            def ldx(k=k, b=b, c=c, hf=hf):
                def mkx(e):
                    a = xn_ag_out.ap().rearrange("(j d) t -> j d t", j=NC)
                    a = a.rearrange("(b s) d t -> b s d t", b=B)
                    npieces = TS // SP
                    s0 = hf * npieces
                    a = a[b, s0:s0 + npieces, k * 128:(k + 1) * 128,
                          c * SP:(c + 1) * SP]
                    a = a.transpose([1, 0, 2])
                    return e.dma_start(
                        out=xns[k][:, :].rearrange("p (s t) -> p s t",
                                                   s=npieces), in_=a)
                S.do("sync", mkx, wait=RDY["xn_ag"] + fget(f"xns{k}"),
                     inc=s_xns, k=16)
                RDY[f"xns{k}"] = s_xns.at()
            pend.append(ldx)

        def mk_sil(it):
            def sil(it=it):
                pp = it % 2
                bg = 2 + 2 * (it % 2)
                bu = 3 + 2 * (it % 2)
                S.do("scalar", lambda e, it=it, pp=pp, bg=bg: e.activation(
                    sg_b[pp][:, :], pb[bg][:, 0:TS], AF.Silu,
                    scale=gs_t[:, it:it + 1]),
                    wait=rget(f"gu{bg}") + RDY["consts"] + fget(f"sg{pp}"),
                    inc=s_sil)
                S.do("scalar", lambda e, it=it, pp=pp, bu=bu: e.activation(
                    usb_b[pp][:, :], pb[bu][:, 0:TS], AF.Copy,
                    scale=us_t[:, it:it + 1]),
                    wait=rget(f"gu{bu}") + s_sil.at() + fget(f"usb{pp}"),
                    inc=s_sil)
                FREE[f"pbank{bg}"] = s_sil.at()
                FREE[f"pbank{bu}"] = s_sil.at()
                S.do("vector", lambda e, it=it, pp=pp: e.tensor_mul(
                    hsub[it][:, :], sg_b[pp][:, :], usb_b[pp][:, :]),
                    wait=s_sil.at() + fget(f"hsub{it}"), inc=s_hs)
                FREE[f"sg{pp}"] = s_hs.at()
                FREE[f"usb{pp}"] = s_hs.at()
                RDY[f"hsub{it}"] = s_hs.at()
            return sil

        for it in range(IT):
            for (boff, wt) in ((2, wg_t), (3, wu_t)):
                bk = boff + 2 * (it % 2)
                for k in range(KT):
                    def gmm(it=it, bk=bk, wt=wt, k=k):
                        w0 = []
                        if k == 0:
                            w0 = RDY["weights"] + fget(f"pbank{bk}")
                            for kk2 in range(KT):
                                w0 = w0 + rget(f"xns{kk2}")
                        S.do("tensor", lambda e, it=it, k=k, wt=wt, bk=bk:
                             e.matmul(pb[bk][:, 0:TS],
                                      wt[k][:, it * 128:(it + 1) * 128],
                                      xns[k][:, :], start=(k == 0),
                                      stop=(k == KT - 1)),
                             wait=w0, inc=s_gu)
                        if k == KT - 1:
                            RDY[f"gu{bk}"] = s_gu.at()
                    pend.append(gmm)
            if it >= 1:
                pend.append(mk_sil(it - 1))
            if it == IT - 1:
                pend.append(mk_sil(it))

        def relx():
            for kk2 in range(KT):
                FREE[f"xns{kk2}"] = s_gu.at()
        pend.append(relx)

    def queue_down_sub(c, b, hf):
        t0 = hf * TS
        npieces = TS // SP

        def mk_evac(hm):
            def ev(hm=hm):
                bk = 6 + (hm % 2)
                ep = hm % 2
                S.do("scalar", lambda e, bk=bk, ep=ep: e.copy(
                    ev_b[ep][:, 0:TS], pb[bk][:, 0:TS]),
                    wait=rget(f"dn{bk}") + fget(f"ev{ep}"), inc=s_oev)
                FREE[f"pbank{bk}"] = s_oev.at()

                def wrs(e, c=c, b=b, hf=hf, hm=hm, ep=ep, npieces=npieces):
                    a = rs_in[c].ap().rearrange("(j d) t -> j d t", j=NC)
                    a = a.rearrange("(b s) d t -> b s d t", b=B)
                    s0 = hf * npieces
                    a = a[b, s0:s0 + npieces, hm * 128:(hm + 1) * 128, :]
                    a = a.transpose([1, 0, 2])
                    return e.dma_start(
                        out=a, in_=ev_b[ep][:, 0:TS].rearrange(
                            "p (s t) -> p s t", s=npieces))

                S.do("sync", wrs, wait=s_oev.at(), inc=sem(f"s_rsw{ep}"), k=16)
                FREE[f"ev{ep}"] = sem(f"s_rsw{ep}").at()
            return ev

        for hm in range(HT):
            bk = 6 + (hm % 2)
            for kk in range(IT):
                def dmm(hm=hm, kk=kk, bk=bk):
                    w0 = []
                    if kk == 0:
                        w0 = RDY["weights"] + fget(f"pbank{bk}")
                        for it2 in range(IT):
                            w0 = w0 + rget(f"hsub{it2}")
                    S.do("tensor", lambda e, hm=hm, kk=kk, bk=bk: e.matmul(
                        pb[bk][:, 0:TS], wd_t[kk][:, hm * 128:(hm + 1) * 128],
                        hsub[kk][:, :], start=(kk == 0), stop=False),
                        wait=w0, inc=s_dn)
                pend.append(dmm)
            for ti in range(DT):
                def omm(hm=hm, ti=ti, bk=bk, b=b, t0=t0):
                    i = b * DT + ti
                    w0 = []
                    if ti == 0:
                        w0 = rget(f"rhs{b * DT}") + rget(f"rhs{b * DT + DT - 1}")
                    S.do("tensor", lambda e, hm=hm, ti=ti, i=i, t0=t0, bk=bk:
                         e.matmul(pb[bk][:, 0:TS],
                                  opT_t[ti][:, hm * 128:(hm + 1) * 128],
                                  rhs_t[i][:, t0:t0 + TS],
                                  start=False, stop=(ti == DT - 1)),
                         wait=w0, inc=s_dn)
                    if ti == DT - 1:
                        RDY[f"dn{bk}"] = s_dn.at()
                pend.append(omm)
            if hm >= 1:
                pend.append(mk_evac(hm - 1))
            if hm == HT - 1:
                pend.append(mk_evac(hm))

        def fin(b=b):
            for it in range(IT):
                FREE[f"hsub{it}"] = s_dn.at()
            FREE["hsub_arena"] = s_dn.at() + s_hs.at()
        pend.append(fin)

    BLEAD = NBB - 1
    GSEQ = [(c_, b_, n_) for c_ in range(NCH) for b_ in range(B)
            for _ti in range(DT) for n_ in range(N)]
    gbc = [0]

    def emit_bcast_global():
        k = gbc[0]
        gbc[0] += 1
        cg, bg, ng = GSEQ[k]
        pbc = k % NBB

        def bbf(e, bg=bg, ng=ng, pbc=pbc, cg=cg):
            a = bc_v[bg, ng, :, cg * SP:(cg + 1) * SP]
            a = a.unsqueeze(0).broadcast_to([128, GB, SP])
            return e.dma_start(
                out=bb_b[pbc][:, :].rearrange("p (s t) -> p s t", s=GB),
                in_=a)

        S.do("gpsimd", bbf, wait=RDY["bc_ag"] + fget(f"bb{pbc}"),
             inc=sem(f"s_bb{pbc}"), k=16)
        RDY[f"bbn{k}"] = sem(f"s_bb{pbc}").at()

        def cbf(e, bg=bg, ng=ng, pbc=pbc):
            a = bc_v[bg, N + ng, 0:nper, 0:SP]
            a = a.unsqueeze(0).broadcast_to([128, nper, SP])
            return e.dma_start(
                out=cb_b[pbc][:, :].rearrange("p (s t) -> p s t", s=nper),
                in_=a)

        S.do("gpsimd", cbf, wait=RDY["bc_ag"] + fget(f"cb{pbc}"),
             inc=sem(f"s_cb{pbc}"), k=16)
        RDY[f"cbn{k}"] = sem(f"s_cb{pbc}").at()

    def emit_scan_b(c, b):
        for ti in range(DT):
            i = b * DT + ti
            for n in range(N):
                kglob = ((c * B + b) * DT + ti) * N + n
                pbc = kglob % NBB
                while gbc[0] <= kglob + BLEAD and gbc[0] < len(GSEQ):
                    emit_bcast_global()
                RDY["bb"] = RDY[f"bbn{kglob}"]
                RDY["cb"] = RDY[f"cbn{kglob}"]
                q = it_scan[0] % 2
                qd = it_scan[0] % NDA
                it_scan[0] += 1
                S.do("scalar", lambda e, i=i, ti=ti, n=n, qd=qd: e.activation(
                    da_b[qd][:, :], dt_ch[i][:, :], AF.Exp,
                    scale=a_scr[:, ti * N + n:ti * N + n + 1]),
                    wait=rget(f"dt_ch{i}") + RDY["a_t"] + fget(f"da{qd}"),
                    inc=s_da)
                S.do("vector", lambda e, i=i, q=q, pbc=pbc: e.tensor_mul(
                    u_b[q][:, :], dx_ch[i][:, :], bb_b[pbc][:, :]),
                    wait=rget(f"dx_ch{i}") + RDY["bb"] + fget(f"u{q}"),
                    inc=s_u)
                FREE[f"bb{pbc}"] = s_u.at()
                if c == 0:
                    init = 0.0
                    winit = []
                else:
                    init = carry[i][:, n:n + 1]
                    winit = rget(f"carry{i}_{n}")
                S.do("vector", lambda e, q=q, qd=qd, init=init:
                     e.tensor_tensor_scan(
                         h_b[q][:, :], da_b[qd][:, :], u_b[q][:, :], init,
                         OP.mult, OP.add),
                     wait=s_da.at() + s_u.at() + fget(f"h{q}") + winit,
                     inc=s_h)
                FREE[f"da{qd}"] = s_h.at()
                FREE[f"u{q}"] = s_h.at()
                ch = s_h.at()
                S.do("gpsimd", lambda e, i=i, n=n, q=q: e.tensor_copy(
                    carry[i][:, n:n + 1], h_b[q][:, TC - 1:TC]),
                    wait=ch, inc=s_cp)
                RDY[f"carry{i}_{n}"] = s_cp.at()

                def zf(e, q=q, pbc=pbc):
                    if REP == 1:
                        return e.tensor_mul(z_b[q][:, :], h_b[q][:, :],
                                            cb_b[pbc][:, :])
                    hh = h_b[q][:, :].rearrange("p (r t) -> p r t", r=REP)
                    zz = z_b[q][:, :].rearrange("p (r t) -> p r t", r=REP)
                    cc2 = cb_b[pbc][:, :].unsqueeze(1).broadcast_to(
                        [128, REP, CHW])
                    return e.tensor_tensor(zz, hh, cc2, OP.mult)

                S.do("vector", zf,
                     wait=ch + RDY["cb"] + fget(f"z{q}"), inc=s_z)
                FREE[f"h{q}"] = s_z.at() + s_cp.at()
                FREE[f"cb{pbc}"] = s_z.at()
                for yh in range(YH):
                    w0 = s_z.at() + (fget(f"pbank{yh}") if n == 0 else [])
                    S.do("tensor", lambda e, q=q, yh=yh, n=n: e.matmul(
                        pb[yh][:, 0:YC], id_bf[:, :],
                        z_b[q][:, yh * YC:(yh + 1) * YC],
                        start=(n == 0), stop=(n == N - 1)),
                        wait=w0, inc=s_ymm)
                FREE[f"z{q}"] = s_ymm.at()
                left = (DT - ti) * N - (n + 1)
                rate = -(-len(pend) // max(1, left)) if left else len(pend)
                pop_pend(rate)
            for yh in range(YH):
                S.do("vector", lambda e, yh=yh: e.tensor_copy(
                    z_b[0][:, yh * YC:(yh + 1) * YC], pb[yh][:, 0:YC]),
                    wait=s_ymm.at() + fget("z0"), inc=s_yev)
                FREE[f"pbank{yh}"] = s_yev.at()
            cy = s_yev.at()
            S.do("vector", lambda e, i=i, ti=ti: e.scalar_tensor_tensor(
                rhs_t[i][:, :], xnd_t[ti][:, :], dpar_t[:, ti:ti + 1],
                z_b[0][:, :], OP.mult, OP.add),
                wait=cy + rget(f"xndt{ti}") + RDY["consts"] + fget(f"rhs{i}"),
                inc=s_rhs)
            FREE["z0"] = s_rhs.at()
            FREE[f"xndt{ti}"] = s_rhs.at()
            RDY[f"rhs{i}"] = s_rhs.at()
            FREE[f"dt_ch{i}"] = s_da.at()
            FREE[f"dx_ch{i}"] = s_u.at()

    def emit_xnd_load(c, b):
        for ti in range(DT):
            def f(e, c=c, b=b, ti=ti):
                a = xn_a2a_out.ap().rearrange("(j dd) t -> j dd t", j=NC)
                a = a.rearrange("(b s) dd t -> b s dd t", b=B)
                a = a[b, :, ti * 128:(ti + 1) * 128, c * SP:(c + 1) * SP]
                a = a.transpose([1, 0, 2])
                return e.dma_start(
                    out=xnd_t[ti][:, :].rearrange("p (s t) -> p s t", s=GB),
                    in_=a)
            sxn = sem(f"s_xnd{ti}")
            S.do("sync", f, wait=RDY["xn_a2a"] + fget(f"xndt{ti}"),
                 inc=sxn, k=16)
            RDY[f"xndt{ti}"] = sxn.at()

    def emit_rs(c):
        S.do("gpsimd", lambda e, c=c: e.collective_compute(
            "ReduceScatter", OP.add, replica_groups=[list(range(NC))],
            ins=[rs_in[c].ap().opt()], outs=[rs_out[c].ap().opt()]),
            wait=sem("s_rsw0").at() + sem("s_rsw1").at(),
            inc=sem(f"s_ccrs{c}"), k=1)
        RDY[f"rs{c}"] = sem(f"s_ccrs{c}").at()

    assert SP <= 256
    s_fvc = sem("s_fvc")

    def fslot(j):
        return ev_b[j % 2][:, (j // 2) * SP:(j // 2 + 1) * SP]

    def queue_final(c):
        for grp in range(D // 512):
            def rb(c=c, grp=grp):
                for j in range(4):
                    dk = grp * 4 + j
                    ep = j % 2
                    S.do("sync", lambda e, c=c, dk=dk, j=j: e.dma_start(
                        out=fslot(j), in_=rs_out[c][dk * 128:(dk + 1) * 128, :]),
                        wait=rget(f"rs{c}") + fget(f"ev{ep}"),
                        inc=sem(f"s_fev{ep}"), k=16)
                RDY["frb"] = sem("s_fev0").at() + sem("s_fev1").at()
            pend.append(rb)

            def tpev(c=c, grp=grp):
                for tb in range(SP // 128):
                    bk = 6 + tb
                    for j in range(4):
                        src = fslot(j)[:, tb * 128:(tb + 1) * 128]
                        S.do("tensor", lambda e, j=j, bk=bk, src=src:
                             e.transpose(pb[bk][:, j * 128:(j + 1) * 128], src,
                                         id_f[:, :]),
                             wait=RDY["frb"] + fget(f"pbank{bk}"), inc=s_ftp)
                ctp = s_ftp.at()
                for tb in range(SP // 128):
                    bk = 6 + tb
                    S.do("scalar", lambda e, tb=tb, bk=bk: e.copy(
                        ev_b[tb][:, 0:512], pb[bk][:, 0:512]),
                        wait=ctp, inc=s_fvc)
                    FREE[f"pbank{bk}"] = s_fvc.at()
                    S.do("gpsimd", lambda e, c=c, grp=grp, tb=tb: e.dma_start(
                        out=y_out[c, tb * 128:(tb + 1) * 128,
                                  grp * 512:(grp + 1) * 512],
                        in_=ev_b[tb][:, 0:512], accum_op=OP.add),
                        wait=s_fvc.at() + RDY["prefill"],
                        inc=sem(f"s_oacc{tb}"), k=16)
                    FREE[f"ev{tb}"] = sem(f"s_oacc{tb}").at()
            pend.append(tpev)

    emit_chunk_loads(0)
    for c in range(NCH):
        for b in range(B):
            emit_xnd_load(c, b)
            queue_ffn_sub(c, b, 0)
            emit_scan_b(c, b)
            if c + 1 < NCH:
                emit_chunk_loads(c + 1, only_b=b)
            if b == 0 and c >= 2:
                queue_final(c - 2)
            if b == 1 and c >= 1:
                emit_rs(c - 1)
            queue_down_sub(c, b, 0)
            for hf in range(1, NSUB):
                queue_ffn_sub(c, b, hf)
                queue_down_sub(c, b, hf)
    pop_pend(len(pend))
    emit_rs(NCH - 1)

    queue_final(NCH - 2)
    queue_final(NCH - 1)
    pop_pend(len(pend))

    end_w = sem("s_oacc0").at() + sem("s_oacc1").at() + s_cp.at()
    for c in range(NCH):
        end_w = end_w + sem(f"s_ccrs{c}").at()
    S.do("sync", None, wait=end_w)
    S.do("gpsimd", None, wait=sem("s_oacc0").at() + sem("s_oacc1").at())

    with allow_ici_all_to_all():
        S.emit(nc)
    return nc


def shard_inputs(cfg, inp):
    D, N, R = cfg["D"], cfg["N"], cfg["R"]
    NC, NCH, TC, SP = cfg["NC"], cfg["NCH"], cfg["TC"], cfg["SP"]
    DOWN, DT, KT, IOWN, IT = (cfg["DOWN"], cfg["DT"], cfg["KT"], cfg["IOWN"],
                              cfg["IT"])
    GB = cfg["GB"]

    x = np.asarray(inp["L0_attn_x"], np.float32)
    gw = np.asarray(inp["L0_ffn_gate_w"], np.int32)
    uw = np.asarray(inp["L0_ffn_up_w"], np.int32)
    dw = np.asarray(inp["L0_ffn_down_w"], np.int32)
    gsc = np.asarray(inp["L0_ffn_gate_scale"], np.float32)
    usc = np.asarray(inp["L0_ffn_up_scale"], np.float32)
    dsc = np.asarray(inp["L0_ffn_down_scale"], np.float32)
    op = np.asarray(inp["L0_attn_out_proj_w"], np.float32)
    xp = np.asarray(inp["L0_attn_x_proj_w"], np.float32)
    dp = np.asarray(inp["L0_attn_dt_proj_w"], np.float32)
    dpb = np.asarray(inp["L0_attn_dt_proj_b"], np.float32)
    nw = np.asarray(inp["L0_attn_norm_w"], np.float32)
    nb = np.asarray(inp["L0_attn_norm_b"], np.float32)
    dpar = np.asarray(inp["L0_attn_D_param"], np.float32)
    alog = np.asarray(inp["L0_attn_A_log"], np.float32)

    def cols(v, nt):
        return np.ascontiguousarray(v.reshape(nt, 128).T)

    ident_bf = np.eye(128, dtype=BFNP)
    ident_f = np.eye(128, dtype=np.float32)
    xpT = np.ascontiguousarray(xp.T)
    dpT = np.ascontiguousarray(dp.T)
    dpb_c, nw_c, nb_c = cols(dpb, KT), cols(nw, KT), cols(nb, KT)

    maps = []
    for k in range(NC):
        b, s = k // GB, k % GB
        xb = x[b]
        stripes = [xb[c * TC + s * SP:c * TC + (s + 1) * SP] for c in range(NCH)]
        Dk = slice(k * DOWN, (k + 1) * DOWN)
        Ik = slice(k * IOWN, (k + 1) * IOWN)
        al = alog[Dk]
        al_t = np.zeros((128, DT * N), np.float32)
        for h in range(DT):
            al_t[:, h * N:(h + 1) * N] = al[h * 128:(h + 1) * 128, :]
        maps.append({
            "x_own": np.ascontiguousarray(np.concatenate(stripes, axis=0)),
            "wgT": np.ascontiguousarray(gw[Ik].T),
            "wuT": np.ascontiguousarray(uw[Ik].T),
            "wdT": np.ascontiguousarray(dw[:, Ik].T),
            "gs": cols(gsc[Ik], IT),
            "us": cols(usc[Ik], IT),
            "ds": np.ascontiguousarray(dsc.reshape(1, D)),
            "opT": np.ascontiguousarray(op[:, Dk].T),
            "xpT": xpT,
            "dpT": dpT,
            "dpb": dpb_c,
            "nw": nw_c,
            "nb": nb_c,
            "dpar": cols(dpar[Dk], DT),
            "alog": al_t,
            "ident_bf": ident_bf,
            "ident_f32": ident_f,
            "fconst": np.tile(np.array(
                [[1e-5, -0.5, -1.0, 1.0, -128.0, 0.0, 2.0, 0.5]], np.float32),
                (128, 1)),
        })
    return maps


def assemble_output(cfg, results):
    B, L, D = cfg["B"], cfg["L"], cfg["D"]
    NC, NCH, TC, SP, GB = cfg["NC"], cfg["NCH"], cfg["TC"], cfg["SP"], cfg["GB"]
    out = np.empty((B, L, D), np.float32)
    for k in range(NC):
        b, s = k // GB, k % GB
        yo = np.asarray(results[k]["y_out"], np.float32).reshape(NCH, SP, D)
        for c in range(NCH):
            out[b, c * TC + s * SP:c * TC + (s + 1) * SP] = yo[c]
    return out


def kernel(**inputs):
    cfg = CFG
    nc = build_program(cfg)
    in_maps = shard_inputs(cfg, inputs)
    res = run_bass_kernel_spmd(nc, in_maps, core_ids=list(range(cfg["NC"])))
    return assemble_output(cfg, res.results)


# version 19
# speedup vs baseline: 1.0785x; 1.0785x over previous
import contextlib
import numpy as np
import ml_dtypes
import concourse.bass as bass
import concourse.mybir as mybir
from concourse.bass_utils import run_bass_kernel_spmd
from concourse.aot_env import allow_ici_all_to_all

F32 = mybir.dt.float32
BF16 = mybir.dt.bfloat16
I32 = mybir.dt.int32
AF = mybir.ActivationFunctionType
OP = mybir.AluOpType
BFNP = ml_dtypes.bfloat16


def make_cfg(D=2048, L=4096, B=2, N=64, R=128, CHUNK=512, I=8192, NC=8):
    c = dict(D=D, L=L, B=B, N=N, R=R, CHUNK=CHUNK, I=I, NC=NC)
    c["GB"] = NC // B
    c["NCH"] = 4
    c["TC"] = L // c["NCH"]
    c["SP"] = c["TC"] // c["GB"]
    c["TCORE"] = c["NCH"] * c["SP"]
    assert c["TCORE"] == B * L // NC
    assert c["SP"] % 128 == 0
    c["DOWN"] = D // NC
    c["DT"] = c["DOWN"] // 128
    c["NT"] = c["DT"] * B
    c["KT"] = D // 128
    c["IOWN"] = I // NC
    c["IT"] = c["IOWN"] // 128
    c["HT"] = D // 128
    c["TS"] = min(512, c["TC"])
    c["NSUB"] = c["TC"] // c["TS"]
    c["CHW"] = min(CHUNK, c["TC"])
    c["REP"] = c["TC"] // c["CHW"]
    c["YC"] = min(512, c["TC"])
    c["YH"] = c["TC"] // c["YC"]
    c["NPROJ"] = R + 2 * N
    c["TPT"] = c["TCORE"] // 128
    c["NSEQ"] = min(512, c["TCORE"])
    c["NA"] = c["TCORE"] // c["NSEQ"]
    assert c["DT"] * c["YH"] * 2 <= 4 + 4
    assert c["DT"] >= 1 and c["IT"] >= 1
    return c


CFG = make_cfg()


class Sem:
    __slots__ = ("h", "v")

    def __init__(self, handle):
        self.h = handle
        self.v = 0

    def at(self):
        return [(self.h, self.v)]


class Sched:
    def __init__(self):
        self.items = []

    def do(self, eng, fn, wait=(), inc=None, k=1):
        byid = {}
        for (h, v) in wait:
            if v > 0 and (id(h) not in byid or byid[id(h)][1] < v):
                byid[id(h)] = (h, v)
        waits = list(byid.values())
        incs = None
        if inc is not None:
            inc.v += k
            incs = (inc.h, k)
        self.items.append((eng, fn, waits, incs))

    def emit(self, nc):
        with nc.Block() as block:
            for name in ("sync", "scalar", "vector", "tensor", "gpsimd"):
                items = [it for it in self.items if it[0] == name]
                if not items:
                    continue

                def body(eng, items=items):
                    hi = {}
                    for _, fn, waits, incs in items:
                        for (h, v) in waits:
                            if hi.get(id(h), 0) < v:
                                eng.wait_ge(h, v)
                                hi[id(h)] = v
                        inst = fn(eng) if fn is not None else None
                        if incs is not None:
                            assert inst is not None
                            inst.then_inc(incs[0], incs[1])

                getattr(block, name)(body)


def build_program(cfg):
    D, L, B, N, R = cfg["D"], cfg["L"], cfg["B"], cfg["N"], cfg["R"]
    I, NC = cfg["I"], cfg["NC"]
    GB, NCH, TC, SP, TCORE = cfg["GB"], cfg["NCH"], cfg["TC"], cfg["SP"], cfg["TCORE"]
    DOWN, DT, NT, KT = cfg["DOWN"], cfg["DT"], cfg["NT"], cfg["KT"]
    IOWN, IT, HT, TS, NSUB = cfg["IOWN"], cfg["IT"], cfg["HT"], cfg["TS"], cfg["NSUB"]
    REP, CHW, YC, YH, NPROJ = (cfg["REP"], cfg["CHW"], cfg["YC"], cfg["YH"],
                               cfg["NPROJ"])
    TPT, NSEQ, NA = cfg["TPT"], cfg["NSEQ"], cfg["NA"]

    nc = bass.Bass("TRN2", target_bir_lowering=False)
    es = nc.ctx

    def din(name, shape, dtype=F32):
        return nc.dram_tensor(name, list(shape), dtype, kind="ExternalInput")

    x_own = din("x_own", [TCORE, D])
    wgT = din("wgT", [D, IOWN], I32)
    wuT = din("wuT", [D, IOWN], I32)
    wdT = din("wdT", [IOWN, D], I32)
    gs_in = din("gs", [128, IT])
    us_in = din("us", [128, IT])
    ds_in = din("ds", [1, D])
    opT = din("opT", [DOWN, D])
    xpT = din("xpT", [D, NPROJ])
    dpT = din("dpT", [R, D])
    dpb_in = din("dpb", [128, KT])
    nw_in = din("nw", [128, KT])
    nb_in = din("nb", [128, KT])
    dpar_in = din("dpar", [128, DT])
    alog_in = din("alog", [128, DT * N])
    id_bf_in = din("ident_bf", [128, 128], BF16)
    id_f_in = din("ident_f32", [128, 128], F32)
    fc_in = din("fconst", [128, 8], F32)

    y_out = nc.dram_tensor("y_out", [NCH, SP, D], F32, kind="ExternalOutput")

    xn_ag_in = nc.dram_tensor("xn_ag_in", [D, TCORE], BF16)
    xn_ag_out = nc.dram_tensor("xn_ag_out", [NC * D, TCORE], BF16)
    bc_ag_in = nc.dram_tensor("bc_ag_in", [2 * N, TCORE], BF16)
    bc_ag_out = nc.dram_tensor("bc_ag_out", [NC * 2 * N, TCORE], BF16)
    dt_a2a_in = nc.dram_tensor("dt_a2a_in", [D, TCORE], BF16)
    dt_a2a_out = nc.dram_tensor("dt_a2a_out", [D, TCORE], BF16)
    dx_a2a_in = nc.dram_tensor("dx_a2a_in", [D, TCORE], BF16)
    dx_a2a_out = nc.dram_tensor("dx_a2a_out", [D, TCORE], BF16)
    xn_a2a_out = nc.dram_tensor("xn_a2a_out", [D, TCORE], BF16)
    rs_in = [nc.dram_tensor(f"rs_in_{c}", [NC * D, SP], F32) for c in range(NCH)]
    rs_out = [nc.dram_tensor(f"rs_out_{c}", [D, SP], F32) for c in range(NCH)]

    def sb(name, shape, dtype):
        return es.enter_context(nc.sbuf_tensor(name, list(shape), dtype))

    a_t = sb("a_t", [128, DT * N], F32)
    a_scr = sb("a_scr", [128, DT * N], F32)
    gs_t = sb("gs_t", [128, IT], F32)
    us_t = sb("us_t", [128, IT], F32)
    opT_t = [sb(f"opT{i}", [128, D], BF16) for i in range(DT)]
    dpb_t = sb("dpb_t", [128, KT], F32)
    nw_t = sb("nw_t", [128, KT], F32)
    nb_t = sb("nb_t", [128, KT], F32)
    dpar_t = sb("dpar_t", [128, DT], F32)
    id_bf = sb("id_bf", [128, 128], BF16)
    id_f = sb("id_f", [128, 128], F32)
    fc_t = sb("fc_t", [128, 8], F32)
    FCONST = [1e-5, -0.5, -1.0, 1.0, -128.0, 0.0, 2.0, 0.5]
    for fi, fv in enumerate(FCONST):
        nc.const_aps.register(F32, fv, fc_t[:, fi:fi + 1])

    pb = [es.enter_context(nc.psum_tensor(f"pb{i}", [128, 512], F32))
          for i in range(8)]
    pb_bf = [p[:, :].bitcast(BF16) for p in pb]

    S = Sched()
    _sems = {}

    def sem(name):
        if name not in _sems:
            _sems[name] = Sem(es.enter_context(nc.semaphore(name)))
        return _sems[name]

    FREE = {}
    RDY = {}

    def fget(nm):
        return FREE.get(nm, [])

    def rget(nm):
        r = RDY.get(nm)
        assert r is not None, f"missing RDY[{nm}]"
        return r

    s_cd = sem("s_cd")
    s_cc0 = sem("s_cc0")
    for (dst, src) in [(gs_t, gs_in), (us_t, us_in), (dpb_t, dpb_in),
                       (nw_t, nw_in), (nb_t, nb_in), (dpar_t, dpar_in),
                       (id_bf, id_bf_in), (id_f, id_f_in), (a_scr, alog_in),
                       (fc_t, fc_in)]:
        S.do("sync", lambda e, d=dst, s=src: e.dma_start(out=d[:, :], in_=s[:, :]),
             inc=s_cd, k=16)
    c_cd = s_cd.at()
    S.do("scalar", lambda e: e.activation(a_t[:, :], a_scr[:, :], AF.Exp),
         wait=c_cd, inc=s_cc0)
    S.do("vector", lambda e: e.tensor_scalar_mul(a_scr[:, :], a_t[:, :], -1.0),
         wait=s_cc0.at(), inc=s_cc0)
    RDY["a_t"] = s_cc0.at()
    RDY["consts"] = c_cd

    esA = es.enter_context(contextlib.ExitStack())

    def sbA(name, shape, dtype):
        return esA.enter_context(nc.sbuf_tensor(name, list(shape), dtype))

    wsm = sbA("wsm", [128, D], F32)
    xpT_t = sbA("xpT_t", [128, KT * NPROJ], BF16)
    dpT_t = sbA("dpT_t", [128, D], BF16)
    xa_xb = [sbA(f"xa_x{p}", [128, D], F32) for p in range(2)]
    xa_st = sbA("xa_st", [128, D], BF16)
    xa_st2 = sbA("xa_st2", [128, D], BF16)
    xa_tr = sbA("xa_tr", [128, 1024], BF16)
    NST = max(1, D // 512)
    xa_stat = sbA("xa_stat", [128, 6 * NST + 6], F32)
    xa_dr = sbA("xa_dr", [128, TCORE], BF16)
    xa_bc = sbA("xa_bc", [128, TCORE], BF16)
    xa_xn2 = [sbA(f"xa_xn2_{k}", [128, TCORE], BF16) for k in range(KT)]
    xa_dtT = [sbA(f"xa_dtT_{k}", [128, TCORE], BF16) for k in range(KT)]
    xa_sp = [sbA(f"xa_sp{p}", [128, NSEQ], F32) for p in range(2)]

    s_sw = sem("s_sw")
    s_swc = sem("s_swc")
    for i in range(DT):
        S.do("sync", lambda e, i=i: e.dma_start(
            out=wsm[:, 0:D], in_=opT[i * 128:(i + 1) * 128, :]),
            wait=fget("wsm"), inc=s_sw, k=16)
        S.do("scalar", lambda e, i=i: e.copy(opT_t[i][:, :], wsm[:, 0:D]),
             wait=s_sw.at(), inc=s_swc)
        FREE["wsm"] = s_swc.at()
    for k in range(KT):
        S.do("sync", lambda e, k=k: e.dma_start(
            out=wsm[:, 0:NPROJ], in_=xpT[k * 128:(k + 1) * 128, :]),
            wait=fget("wsm"), inc=s_sw, k=16)
        S.do("scalar", lambda e, k=k: e.copy(
            xpT_t[:, k * NPROJ:(k + 1) * NPROJ], wsm[:, 0:NPROJ]),
            wait=s_sw.at(), inc=s_swc)
        FREE["wsm"] = s_swc.at()
    S.do("sync", lambda e: e.dma_start(out=wsm[:, 0:D], in_=dpT[:, :]),
         wait=fget("wsm"), inc=s_sw, k=16)
    S.do("scalar", lambda e: e.copy(dpT_t[:, :], wsm[:, 0:D]),
         wait=s_sw.at(), inc=s_swc)
    RDY["smallw"] = s_swc.at()

    s_xd = sem("s_xd")
    s_ln = sem("s_ln")
    s_tp = sem("s_tp")
    s_tpe = sem("s_tpe")
    s_agw = sem("s_agw")
    s_cc = sem("s_cc")
    TRT = max(1, D // 1024)

    for ti in range(TPT):
        pX = ti % 2
        S.do("sync", lambda e, ti=ti, pX=pX: e.dma_start(
            out=xa_xb[pX][:, :], in_=x_own[ti * 128:(ti + 1) * 128, :]),
            wait=fget(f"xax{pX}"), inc=s_xd, k=16)
        cx = s_xd.at()

        def ln_pass(src_ap, dst, wait0, wfree):
            st6 = xa_stat[:, 0:6 * NST]
            for j in range(NST):
                w = min(512, D)
                S.do("vector", lambda e, j=j, s=src_ap, w=w: e.bn_stats(
                    st6[:, 6 * j:6 * (j + 1)], s[:, w * j:w * (j + 1)]),
                    wait=wait0, inc=s_ln)
            mv = xa_stat[:, 6 * NST:6 * NST + 2]
            S.do("vector", lambda e, mv=mv: e.bn_aggr(mv, st6),
                 wait=s_ln.at(), inc=s_ln)
            lnv = xa_stat[:, 6 * NST + 4:6 * NST + 5]
            rstd = xa_stat[:, 6 * NST + 2:6 * NST + 3]
            nmr = xa_stat[:, 6 * NST + 3:6 * NST + 4]
            S.do("scalar", lambda e, mv=mv, lv=lnv: e.activation(
                lv, mv[:, 1:2], AF.Ln, bias=1e-5),
                wait=s_ln.at() + RDY["consts"], inc=s_ln)
            S.do("scalar", lambda e, lv=lnv, r=rstd: e.activation(
                r, lv, AF.Exp, scale=-0.5), wait=s_ln.at(), inc=s_ln)
            S.do("vector", lambda e, mv=mv, r=rstd, nm=nmr:
                 e.scalar_tensor_tensor(nm, mv[:, 0:1], -1.0, r, OP.mult,
                                        OP.mult),
                 wait=s_ln.at(), inc=s_ln)
            S.do("scalar", lambda e, s=src_ap, d=dst, r=rstd, nm=nmr:
                 e.activation(d[:, :], s[:, :], AF.Identity, scale=r, bias=nm),
                 wait=s_ln.at() + wfree, inc=s_ln)
            return s_ln.at()

        c_xn = ln_pass(xa_xb[pX], xa_st, cx, fget("xa_st"))
        FREE[f"xax{pX}"] = c_xn
        c_xnn = ln_pass(xa_st, xa_st2, c_xn, fget("xa_st2"))

        for (src, kind, cready) in ((xa_st, "xn", c_xn), (xa_st2, "xnn", c_xnn)):
            for half in range(TRT):
                bk = 4 + (half % 2)
                bank = pb_bf[bk]
                nj = min(8, D // 128)
                for j in range(nj):
                    kk = half * 8 + j
                    S.do("tensor", lambda e, s=src, kk=kk, j=j, bank=bank:
                         e.transpose(bank[:, j * 128:(j + 1) * 128],
                                     s[:, kk * 128:(kk + 1) * 128], id_bf[:, :]),
                         wait=cready + RDY["consts"] + fget(f"pbank{bk}"),
                         inc=s_tp)
                c_t = s_tp.at()
                if kind == "xn":
                    S.do("scalar", lambda e, bank=bank, nj=nj: e.copy(
                        xa_tr[:, 0:nj * 128], bank[:, 0:nj * 128]),
                        wait=c_t + fget("xa_tr"), inc=s_tpe)
                    FREE[f"pbank{bk}"] = s_tpe.at()
                    S.do("sync", lambda e, ti=ti, half=half, nj=nj: e.dma_start(
                        out=xn_ag_in.ap().rearrange("(a p) t -> p a t", p=128)
                        [:, half * 8:half * 8 + nj, ti * 128:(ti + 1) * 128],
                        in_=xa_tr[:, 0:nj * 128].rearrange(
                            "p (a t) -> p a t", a=nj)),
                        wait=s_tpe.at(), inc=s_agw, k=16)
                    FREE["xa_tr"] = s_agw.at()
                else:
                    for j in range(nj):
                        kk = half * 8 + j
                        S.do("scalar", lambda e, kk=kk, j=j, bank=bank, ti=ti:
                             e.activation(
                                 xa_xn2[kk][:, ti * 128:(ti + 1) * 128],
                                 bank[:, j * 128:(j + 1) * 128], AF.Identity,
                                 scale=nw_t[:, kk:kk + 1],
                                 bias=nb_t[:, kk:kk + 1]),
                             wait=c_t + RDY["consts"], inc=s_tpe)
                    FREE[f"pbank{bk}"] = s_tpe.at()
            if kind == "xn":
                FREE["xa_st"] = s_tp.at() + c_xnn
            else:
                FREE["xa_st2"] = s_tp.at()
    c_xn2 = s_tpe.at()

    s_pref = sem("s_pref")
    for c_ in range(NCH):
        S.do("sync", lambda e, c_=c_: e.dma_start(
            out=y_out[c_, :, :], in_=x_own[c_ * SP:(c_ + 1) * SP, :]),
            inc=s_pref, k=16)
    RDY["prefill"] = s_pref.at()

    s_ccxn = sem("s_ccxn")
    s_ccxa = sem("s_ccxa")
    S.do("gpsimd", lambda e: e.collective_compute(
        "AllGather", OP.bypass, replica_groups=[list(range(NC))],
        ins=[xn_ag_in.ap().opt()], outs=[xn_ag_out.ap().opt()]),
        wait=s_agw.at(), inc=s_ccxn, k=1)
    RDY["xn_ag"] = s_ccxn.at()
    S.do("gpsimd", lambda e: e.collective_compute(
        "AllToAll", OP.bypass, replica_groups=[list(range(NC))],
        ins=[xn_ag_in.ap().opt()], outs=[xn_a2a_out.ap().opt()]),
        wait=s_agw.at(), inc=s_ccxa, k=1)
    RDY["xn_a2a"] = s_ccxa.at()

    s_mm = sem("s_mm")
    s_ev = sem("s_ev")
    s_dw = sem("s_dw")
    for na in range(NA):
        for m in range(NPROJ // 128):
            bank = pb[m]
            for k in range(KT):
                w0 = (c_xn2 + RDY["smallw"] + fget(f"pbank{m}")) if k == 0 else []
                S.do("tensor", lambda e, na=na, m=m, k=k, bank=bank: e.matmul(
                    bank[:, 0:NSEQ],
                    xpT_t[:, k * NPROJ + m * 128:k * NPROJ + (m + 1) * 128],
                    xa_xn2[k][:, na * NSEQ:(na + 1) * NSEQ],
                    start=(k == 0), stop=(k == KT - 1)),
                    wait=w0, inc=s_mm)
        S.do("scalar", lambda e, na=na: e.copy(
            xa_dr[:, na * NSEQ:(na + 1) * NSEQ], pb[0][:, 0:NSEQ]),
            wait=s_mm.at(), inc=s_ev)
        S.do("scalar", lambda e, na=na: e.copy(
            xa_bc[:, na * NSEQ:(na + 1) * NSEQ], pb[1][:, 0:NSEQ]),
            wait=s_mm.at(), inc=s_ev)
        FREE["pbank0"] = s_ev.at()
        FREE["pbank1"] = s_ev.at()
    c_proj = s_ev.at()
    s_bcw = sem("s_bcw")
    s_ccbc = sem("s_ccbc")
    S.do("sync", lambda e: e.dma_start(out=bc_ag_in.ap(), in_=xa_bc[0:2 * N, :]),
         wait=c_proj, inc=s_bcw, k=16)
    S.do("gpsimd", lambda e: e.collective_compute(
        "AllGather", OP.bypass, replica_groups=[list(range(NC))],
        ins=[bc_ag_in.ap().opt()], outs=[bc_ag_out.ap().opt()]),
        wait=s_bcw.at(), inc=s_ccbc, k=1)
    RDY["bc_ag"] = s_ccbc.at()

    s_dx = sem("s_dx")
    zb, rl = xa_sp[0], xa_sp[1]
    a1 = wsm[:, 0:NSEQ]
    a2 = wsm[:, NSEQ:2 * NSEQ]
    for m in range(KT):
        for na in range(NA):
            bk = 4 + ((m * NA + na) % 2)
            bank = pb[bk]
            S.do("tensor", lambda e, m=m, na=na, bank=bank: e.matmul(
                bank[:, 0:NSEQ], dpT_t[:, m * 128:(m + 1) * 128],
                xa_dr[:, na * NSEQ:(na + 1) * NSEQ], start=True, stop=True),
                wait=c_proj + RDY["smallw"] + fget(f"pbank{bk}"), inc=s_mm)
            S.do("scalar", lambda e, m=m, bank=bank: e.activation(
                zb[:, :], bank[:, 0:NSEQ], AF.Identity,
                bias=dpb_t[:, m:m + 1]),
                wait=s_mm.at() + fget("sp"), inc=s_ev)
            FREE[f"pbank{bk}"] = s_ev.at()
            S.do("scalar", lambda e: e.activation(a1[:, :], zb[:, :], AF.Abs),
                 wait=s_ev.at() + fget("wsm"), inc=s_ev)
            S.do("scalar", lambda e: e.activation(
                a2[:, :], a1[:, :], AF.Exp, scale=-1.0),
                wait=s_ev.at() + RDY["consts"], inc=s_ev)
            S.do("scalar", lambda e: e.activation(
                a1[:, :], a2[:, :], AF.Ln, bias=1.0), wait=s_ev.at(), inc=s_ev)
            S.do("scalar", lambda e: e.activation(rl[:, :], zb[:, :], AF.Relu),
                 wait=s_ev.at(), inc=s_ev)
            S.do("vector", lambda e, m=m, na=na: e.tensor_add(
                xa_dtT[m][:, na * NSEQ:(na + 1) * NSEQ], rl[:, :], a1[:, :]),
                wait=s_ev.at(), inc=s_ev)
            FREE["sp"] = s_ev.at()
            FREE["wsm"] = s_ev.at()
    c_dt = s_ev.at()
    s_dtw = sem("s_dtw")
    s_dxw = sem("s_dxw")
    s_ccdt = sem("s_ccdt")
    s_ccdx = sem("s_ccdx")
    for m in range(KT):
        S.do("sync", lambda e, m=m: e.dma_start(
            out=dt_a2a_in[m * 128:(m + 1) * 128, :], in_=xa_dtT[m][:, :]),
            wait=c_dt, inc=s_dtw, k=16)
        S.do("vector", lambda e, m=m: e.tensor_mul(
            xa_dr[:, :], xa_dtT[m][:, :], xa_xn2[m][:, :]),
            wait=c_dt + fget("xa_dr"), inc=s_dx)
        S.do("sync", lambda e, m=m: e.dma_start(
            out=dx_a2a_in[m * 128:(m + 1) * 128, :], in_=xa_dr[:, :]),
            wait=s_dx.at(), inc=s_dxw, k=16)
        FREE["xa_dr"] = s_dxw.at()
    c_a2aw = s_dtw.at() + s_dxw.at()
    S.do("gpsimd", lambda e: e.collective_compute(
        "AllToAll", OP.bypass, replica_groups=[list(range(NC))],
        ins=[dt_a2a_in.ap().opt()], outs=[dt_a2a_out.ap().opt()]),
        wait=c_a2aw, inc=s_ccdt, k=1)
    RDY["dt_a2a"] = s_ccdt.at()
    S.do("gpsimd", lambda e: e.collective_compute(
        "AllToAll", OP.bypass, replica_groups=[list(range(NC))],
        ins=[dx_a2a_in.ap().opt()], outs=[dx_a2a_out.ap().opt()]),
        wait=c_a2aw, inc=s_ccdx, k=1)
    RDY["dx_a2a"] = s_ccdx.at()

    esA.close()

    dsb = sb("dsb", [128, D], BF16)
    dt_ch = [sb(f"dt_ch{i}", [128, TC], BF16) for i in range(NT)]
    dx_ch = [sb(f"dx_ch{i}", [128, TC], BF16) for i in range(NT)]
    NDA = 5
    da_b = [sb(f"da{i}", [128, TC], BF16) for i in range(NDA)]
    u_b = [sb("u0", [128, TC], BF16)] * 2
    h_b = [sb(f"h{i}", [128, TC], BF16) for i in range(2)]
    z_b = [sb(f"z{i}", [128, TC], BF16) for i in range(2)]
    NBB = 4
    bb_b = [sb(f"bb{i}", [128, TC], BF16) for i in range(NBB)]
    xnd_t = [sb(f"xnd{i}", [128, TC], BF16) for i in range(2)]
    cb_b = [sb(f"cb{i}", [128, CHW], BF16) for i in range(NBB)]
    carry = [sb(f"carry{i}", [128, N], BF16) for i in range(NT)]
    rhs_t = [sb(f"rhs{i}", [128, TC], BF16) for i in range(NT)]
    xns = [sb(f"xns_{k}", [128, TS], BF16) for k in range(KT)]
    hsub_a = sb("hsub_a", [128, IT * TS], BF16)
    hsub = [hsub_a[:, i * TS:(i + 1) * TS] for i in range(IT)]
    sg_b = [sb(f"sg{p}", [128, TS], BF16) for p in range(2)]
    usb_b = [sb(f"usb{p}", [128, TS], BF16) for p in range(2)]
    ev_b = [sb(f"ev{i}", [128, 512], F32) for i in range(2)]

    warena = sb("warena", [128, 2 * KT * IOWN + IT * D], BF16)
    wg_t = [warena[:, k * IOWN:(k + 1) * IOWN] for k in range(KT)]
    wu_t = [warena[:, (KT + k) * IOWN:(KT + k + 1) * IOWN] for k in range(KT)]
    wd_t = [warena[:, 2 * KT * IOWN + k * D:2 * KT * IOWN + (k + 1) * D]
            for k in range(IT)]
    WSW = min(512, max(IOWN, D))
    wstage = [sb(f"wstage{p}", [128, WSW], I32) for p in range(2)]

    s_dsb = sem("s_dsb")
    S.do("gpsimd", lambda e: e.dma_start(
        out=dsb[:, :], in_=ds_in.ap().partition_broadcast(128)),
        wait=c_a2aw, inc=s_dsb, k=16)
    RDY["dsb"] = s_dsb.at()

    s_wd = [sem("s_wd0"), sem("s_wd1")]
    s_wc = sem("s_wc")
    s_wm = sem("s_wm")
    n_w = 0
    for (src, tiles, nt_, is_down, width) in (
            (wgT, wg_t, KT, False, IOWN), (wuT, wu_t, KT, False, IOWN),
            (wdT, wd_t, IT, True, D)):
        for k in range(nt_):
            for h0 in range(0, width, WSW):
                w = min(WSW, width - h0)
                p = n_w % 2
                w0 = fget(f"wstage{p}")
                if n_w == 0:
                    w0 = w0 + c_a2aw + s_dx.at()
                S.do("sync", lambda e, src=src, k=k, p=p, h0=h0, w=w:
                     e.dma_start(out=wstage[p][:, 0:w],
                                 in_=src[k * 128:(k + 1) * 128, h0:h0 + w]),
                     wait=w0, inc=s_wd[p], k=16)
                if not is_down:
                    S.do("scalar", lambda e, tiles=tiles, k=k, p=p, h0=h0, w=w:
                         e.activation(tiles[k][:, h0:h0 + w],
                                      wstage[p][:, 0:w], AF.Copy, bias=-128.0),
                         wait=s_wd[p].at() + RDY["consts"], inc=s_wc)
                    FREE[f"wstage{p}"] = s_wc.at()
                else:
                    for t0 in range(0, w, TS):
                        tw = min(TS, w - t0)
                        pp = (h0 // WSW + t0 // TS) % 2
                        S.do("scalar", lambda e, p=p, t0=t0, tw=tw, pp=pp:
                             e.activation(usb_b[pp][:, 0:tw],
                                          wstage[p][:, t0:t0 + tw], AF.Copy,
                                          bias=-128.0),
                             wait=s_wd[p].at() + fget(f"usb{pp}") +
                             RDY["consts"], inc=s_wc)
                        S.do("vector", lambda e, tiles=tiles, k=k, h0=h0,
                             t0=t0, tw=tw, pp=pp: e.tensor_mul(
                                 tiles[k][:, h0 + t0:h0 + t0 + tw],
                                 usb_b[pp][:, 0:tw],
                                 dsb[:, h0 + t0:h0 + t0 + tw]),
                             wait=s_wc.at() + RDY["dsb"], inc=s_wm)
                        FREE[f"usb{pp}"] = s_wm.at()
                    FREE[f"wstage{p}"] = s_wc.at()
                n_w += 1
    RDY["weights"] = s_wc.at() + s_wm.at()

    s_sc = sem("s_sc")
    nper = CHW // SP
    bc_v = bc_ag_out.ap().rearrange("(b s r) t -> b r s t", b=B, s=GB)

    s_bb = sem("s_bb")
    s_da = sem("s_da")
    s_u = sem("s_u")
    s_h = sem("s_h")
    s_z = sem("s_z")
    s_ymm = sem("s_ymm")
    s_yev = sem("s_yev")
    s_rhs = sem("s_rhs")
    s_cp = sem("s_cp")
    s_xns = sem("s_xns")
    s_gu = sem("s_gu")
    s_sil = sem("s_sil")
    s_hs = sem("s_hs")
    s_dn = sem("s_dn")
    s_oev = sem("s_oev")
    s_orw = sem("s_orw")
    s_rs = sem("s_rs")
    s_fd = sem("s_fd")
    s_ftp = sem("s_ftp")
    s_fa = sem("s_fa")
    s_out = sem("s_out")

    it_scan = [0]
    pend = []

    def pop_pend(k):
        for _ in range(min(k, len(pend))):
            pend.pop(0)()

    def emit_chunk_loads(c, only_b=None):
        for i in range(NT):
            b, half = i // DT, i % DT
            if only_b is not None and b != only_b:
                continue

            def mk(dst, srcd, b=b, half=half, c=c):
                def f(e):
                    a = srcd.ap().rearrange("(j dd) t -> j dd t", j=NC)
                    a = a.rearrange("(b s) dd t -> b s dd t", b=B)
                    a = a[b, :, half * 128:(half + 1) * 128,
                          c * SP:(c + 1) * SP]
                    a = a.transpose([1, 0, 2])
                    return e.dma_start(
                        out=dst[:, :].rearrange("p (s t) -> p s t", s=GB),
                        in_=a)
                return f

            sdt = sem(f"s_dtch{i}")
            sdx = sem(f"s_dxch{i}")
            S.do("sync", mk(dt_ch[i], dt_a2a_out),
                 wait=RDY["dt_a2a"] + fget(f"dt_ch{i}"), inc=sdt, k=16)
            RDY[f"dt_ch{i}"] = sdt.at()
            S.do("sync", mk(dx_ch[i], dx_a2a_out),
                 wait=RDY["dx_a2a"] + fget(f"dx_ch{i}"), inc=sdx, k=16)
            RDY[f"dx_ch{i}"] = sdx.at()


    def queue_ffn_sub(c, b, hf):
        for k in range(KT):
            def ldx(k=k, b=b, c=c, hf=hf):
                def mkx(e):
                    a = xn_ag_out.ap().rearrange("(j d) t -> j d t", j=NC)
                    a = a.rearrange("(b s) d t -> b s d t", b=B)
                    npieces = TS // SP
                    s0 = hf * npieces
                    a = a[b, s0:s0 + npieces, k * 128:(k + 1) * 128,
                          c * SP:(c + 1) * SP]
                    a = a.transpose([1, 0, 2])
                    return e.dma_start(
                        out=xns[k][:, :].rearrange("p (s t) -> p s t",
                                                   s=npieces), in_=a)
                S.do("sync", mkx, wait=RDY["xn_ag"] + fget(f"xns{k}"),
                     inc=s_xns, k=16)
                RDY[f"xns{k}"] = s_xns.at()
            pend.append(ldx)

        def mk_sil(it):
            def sil(it=it):
                pp = it % 2
                bg = 2 + 2 * (it % 2)
                bu = 3 + 2 * (it % 2)
                S.do("scalar", lambda e, it=it, pp=pp, bg=bg: e.activation(
                    sg_b[pp][:, :], pb[bg][:, 0:TS], AF.Silu,
                    scale=gs_t[:, it:it + 1]),
                    wait=rget(f"gu{bg}") + RDY["consts"] + fget(f"sg{pp}"),
                    inc=s_sil)
                S.do("scalar", lambda e, it=it, pp=pp, bu=bu: e.activation(
                    usb_b[pp][:, :], pb[bu][:, 0:TS], AF.Copy,
                    scale=us_t[:, it:it + 1]),
                    wait=rget(f"gu{bu}") + s_sil.at() + fget(f"usb{pp}"),
                    inc=s_sil)
                FREE[f"pbank{bg}"] = s_sil.at()
                FREE[f"pbank{bu}"] = s_sil.at()
                S.do("vector", lambda e, it=it, pp=pp: e.tensor_mul(
                    hsub[it][:, :], sg_b[pp][:, :], usb_b[pp][:, :]),
                    wait=s_sil.at() + fget(f"hsub{it}"), inc=s_hs)
                FREE[f"sg{pp}"] = s_hs.at()
                FREE[f"usb{pp}"] = s_hs.at()
                RDY[f"hsub{it}"] = s_hs.at()
            return sil

        for it in range(IT):
            for (boff, wt) in ((2, wg_t), (3, wu_t)):
                bk = boff + 2 * (it % 2)
                for k in range(KT):
                    def gmm(it=it, bk=bk, wt=wt, k=k):
                        w0 = []
                        if k == 0:
                            w0 = RDY["weights"] + fget(f"pbank{bk}")
                            for kk2 in range(KT):
                                w0 = w0 + rget(f"xns{kk2}")
                        S.do("tensor", lambda e, it=it, k=k, wt=wt, bk=bk:
                             e.matmul(pb[bk][:, 0:TS],
                                      wt[k][:, it * 128:(it + 1) * 128],
                                      xns[k][:, :], start=(k == 0),
                                      stop=(k == KT - 1)),
                             wait=w0, inc=s_gu)
                        if k == KT - 1:
                            RDY[f"gu{bk}"] = s_gu.at()
                    pend.append(gmm)
            if it >= 1:
                pend.append(mk_sil(it - 1))
            if it == IT - 1:
                pend.append(mk_sil(it))

        def relx():
            for kk2 in range(KT):
                FREE[f"xns{kk2}"] = s_gu.at()
        pend.append(relx)

    def queue_down_sub(c, b, hf):
        t0 = hf * TS
        npieces = TS // SP

        def mk_evac(hm):
            def ev(hm=hm):
                bk = 6 + (hm % 2)
                ep = hm % 2
                S.do("scalar", lambda e, bk=bk, ep=ep: e.copy(
                    ev_b[ep][:, 0:TS], pb[bk][:, 0:TS]),
                    wait=rget(f"dn{bk}") + fget(f"ev{ep}"), inc=s_oev)
                FREE[f"pbank{bk}"] = s_oev.at()

                def wrs(e, c=c, b=b, hf=hf, hm=hm, ep=ep, npieces=npieces):
                    a = rs_in[c].ap().rearrange("(j d) t -> j d t", j=NC)
                    a = a.rearrange("(b s) d t -> b s d t", b=B)
                    s0 = hf * npieces
                    a = a[b, s0:s0 + npieces, hm * 128:(hm + 1) * 128, :]
                    a = a.transpose([1, 0, 2])
                    return e.dma_start(
                        out=a, in_=ev_b[ep][:, 0:TS].rearrange(
                            "p (s t) -> p s t", s=npieces))

                S.do("sync", wrs, wait=s_oev.at(), inc=sem(f"s_rsw{ep}"), k=16)
                FREE[f"ev{ep}"] = sem(f"s_rsw{ep}").at()
            return ev

        for hm in range(HT):
            bk = 6 + (hm % 2)

            def grp_mm(hm=hm, bk=bk, b=b, t0=t0):
                for kk in range(IT):
                    w0 = []
                    if kk == 0:
                        w0 = RDY["weights"] + fget(f"pbank{bk}")
                        for it2 in range(IT):
                            w0 = w0 + rget(f"hsub{it2}")
                    S.do("tensor", lambda e, hm=hm, kk=kk, bk=bk: e.matmul(
                        pb[bk][:, 0:TS], wd_t[kk][:, hm * 128:(hm + 1) * 128],
                        hsub[kk][:, :], start=(kk == 0), stop=False),
                        wait=w0, inc=s_dn)
                for ti in range(DT):
                    i = b * DT + ti
                    w0 = []
                    if ti == 0:
                        w0 = rget(f"rhs{b * DT}") + rget(f"rhs{b * DT + DT - 1}")
                    S.do("tensor", lambda e, hm=hm, ti=ti, i=i, t0=t0, bk=bk:
                         e.matmul(pb[bk][:, 0:TS],
                                  opT_t[ti][:, hm * 128:(hm + 1) * 128],
                                  rhs_t[i][:, t0:t0 + TS],
                                  start=False, stop=(ti == DT - 1)),
                         wait=w0, inc=s_dn)
                    if ti == DT - 1:
                        RDY[f"dn{bk}"] = s_dn.at()
                FREE[f"pbank{bk}"] = [(s_oev.h,
                                       s_oev.v + (1 if hm == 0 else 2))]
            pend.append(grp_mm)
            if hm >= 1:
                pend.append(mk_evac(hm - 1))
            if hm == HT - 1:
                pend.append(mk_evac(hm))

        def fin(b=b):
            for it in range(IT):
                FREE[f"hsub{it}"] = s_dn.at()
            FREE["hsub_arena"] = s_dn.at() + s_hs.at()
        pend.append(fin)

    BLEAD = NBB - 1
    GSEQ = [(c_, b_, n_) for c_ in range(NCH) for b_ in range(B)
            for _ti in range(DT) for n_ in range(N)]
    gbc = [0]

    def emit_bcast_global():
        k = gbc[0]
        gbc[0] += 1
        cg, bg, ng = GSEQ[k]
        pbc = k % NBB

        def bbf(e, bg=bg, ng=ng, pbc=pbc, cg=cg):
            a = bc_v[bg, ng, :, cg * SP:(cg + 1) * SP]
            a = a.unsqueeze(0).broadcast_to([128, GB, SP])
            return e.dma_start(
                out=bb_b[pbc][:, :].rearrange("p (s t) -> p s t", s=GB),
                in_=a)

        S.do("gpsimd", bbf, wait=RDY["bc_ag"] + fget(f"bb{pbc}"),
             inc=sem(f"s_bb{pbc}"), k=16)
        RDY[f"bbn{k}"] = sem(f"s_bb{pbc}").at()

        def cbf(e, bg=bg, ng=ng, pbc=pbc):
            a = bc_v[bg, N + ng, 0:nper, 0:SP]
            a = a.unsqueeze(0).broadcast_to([128, nper, SP])
            return e.dma_start(
                out=cb_b[pbc][:, :].rearrange("p (s t) -> p s t", s=nper),
                in_=a)

        S.do("gpsimd", cbf, wait=RDY["bc_ag"] + fget(f"cb{pbc}"),
             inc=sem(f"s_cb{pbc}"), k=16)
        RDY[f"cbn{k}"] = sem(f"s_cb{pbc}").at()

    def emit_scan_b(c, b):
        for ti in range(DT):
            i = b * DT + ti
            for n in range(N):
                kglob = ((c * B + b) * DT + ti) * N + n
                pbc = kglob % NBB
                while gbc[0] <= kglob + BLEAD and gbc[0] < len(GSEQ):
                    emit_bcast_global()
                RDY["bb"] = RDY[f"bbn{kglob}"]
                RDY["cb"] = RDY[f"cbn{kglob}"]
                q = it_scan[0] % 2
                qd = it_scan[0] % NDA
                it_scan[0] += 1
                S.do("scalar", lambda e, i=i, ti=ti, n=n, qd=qd: e.activation(
                    da_b[qd][:, :], dt_ch[i][:, :], AF.Exp,
                    scale=a_scr[:, ti * N + n:ti * N + n + 1]),
                    wait=rget(f"dt_ch{i}") + RDY["a_t"] + fget(f"da{qd}"),
                    inc=s_da)
                S.do("vector", lambda e, i=i, q=q, pbc=pbc: e.tensor_mul(
                    u_b[q][:, :], dx_ch[i][:, :], bb_b[pbc][:, :]),
                    wait=rget(f"dx_ch{i}") + RDY["bb"] + fget(f"u{q}"),
                    inc=s_u)
                FREE[f"bb{pbc}"] = s_u.at()
                if c == 0:
                    init = 0.0
                    winit = []
                else:
                    init = carry[i][:, n:n + 1]
                    winit = rget(f"carry{i}_{n}")
                S.do("vector", lambda e, q=q, qd=qd, init=init:
                     e.tensor_tensor_scan(
                         h_b[q][:, :], da_b[qd][:, :], u_b[q][:, :], init,
                         OP.mult, OP.add),
                     wait=s_da.at() + s_u.at() + fget(f"h{q}") + winit,
                     inc=s_h)
                FREE[f"da{qd}"] = s_h.at()
                FREE[f"u{q}"] = s_h.at()
                ch = s_h.at()
                S.do("gpsimd", lambda e, i=i, n=n, q=q: e.tensor_copy(
                    carry[i][:, n:n + 1], h_b[q][:, TC - 1:TC]),
                    wait=ch, inc=s_cp)
                RDY[f"carry{i}_{n}"] = s_cp.at()

                def zf(e, q=q, pbc=pbc):
                    if REP == 1:
                        return e.tensor_mul(z_b[q][:, :], h_b[q][:, :],
                                            cb_b[pbc][:, :])
                    hh = h_b[q][:, :].rearrange("p (r t) -> p r t", r=REP)
                    zz = z_b[q][:, :].rearrange("p (r t) -> p r t", r=REP)
                    cc2 = cb_b[pbc][:, :].unsqueeze(1).broadcast_to(
                        [128, REP, CHW])
                    return e.tensor_tensor(zz, hh, cc2, OP.mult)

                S.do("vector", zf,
                     wait=ch + RDY["cb"] + fget(f"z{q}"), inc=s_z)
                FREE[f"h{q}"] = s_z.at() + s_cp.at()
                FREE[f"cb{pbc}"] = s_z.at()
                for yh in range(YH):
                    w0 = s_z.at() + (fget(f"pbank{yh}") if n == 0 else [])
                    S.do("tensor", lambda e, q=q, yh=yh, n=n: e.matmul(
                        pb[yh][:, 0:YC], id_bf[:, :],
                        z_b[q][:, yh * YC:(yh + 1) * YC],
                        start=(n == 0), stop=(n == N - 1)),
                        wait=w0, inc=s_ymm)
                FREE[f"z{q}"] = s_ymm.at()
                left = (DT - ti) * N - (n + 1)
                rate = -(-len(pend) // max(1, left)) if left else len(pend)
                pop_pend(rate)
            for yh in range(YH):
                S.do("vector", lambda e, yh=yh: e.tensor_copy(
                    z_b[0][:, yh * YC:(yh + 1) * YC], pb[yh][:, 0:YC]),
                    wait=s_ymm.at() + fget("z0"), inc=s_yev)
                FREE[f"pbank{yh}"] = s_yev.at()
            cy = s_yev.at()
            S.do("vector", lambda e, i=i, ti=ti: e.scalar_tensor_tensor(
                rhs_t[i][:, :], xnd_t[ti][:, :], dpar_t[:, ti:ti + 1],
                z_b[0][:, :], OP.mult, OP.add),
                wait=cy + rget(f"xndt{ti}") + RDY["consts"] + fget(f"rhs{i}"),
                inc=s_rhs)
            FREE["z0"] = s_rhs.at()
            FREE[f"xndt{ti}"] = s_rhs.at()
            RDY[f"rhs{i}"] = s_rhs.at()
            FREE[f"dt_ch{i}"] = s_da.at()
            FREE[f"dx_ch{i}"] = s_u.at()

    def emit_xnd_load(c, b):
        for ti in range(DT):
            def f(e, c=c, b=b, ti=ti):
                a = xn_a2a_out.ap().rearrange("(j dd) t -> j dd t", j=NC)
                a = a.rearrange("(b s) dd t -> b s dd t", b=B)
                a = a[b, :, ti * 128:(ti + 1) * 128, c * SP:(c + 1) * SP]
                a = a.transpose([1, 0, 2])
                return e.dma_start(
                    out=xnd_t[ti][:, :].rearrange("p (s t) -> p s t", s=GB),
                    in_=a)
            sxn = sem(f"s_xnd{ti}")
            S.do("sync", f, wait=RDY["xn_a2a"] + fget(f"xndt{ti}"),
                 inc=sxn, k=16)
            RDY[f"xndt{ti}"] = sxn.at()

    def emit_rs(c):
        S.do("gpsimd", lambda e, c=c: e.collective_compute(
            "ReduceScatter", OP.add, replica_groups=[list(range(NC))],
            ins=[rs_in[c].ap().opt()], outs=[rs_out[c].ap().opt()]),
            wait=sem("s_rsw0").at() + sem("s_rsw1").at(),
            inc=sem(f"s_ccrs{c}"), k=1)
        RDY[f"rs{c}"] = sem(f"s_ccrs{c}").at()

    assert SP <= 256
    s_fvc = sem("s_fvc")

    def fslot(j):
        return ev_b[j % 2][:, (j // 2) * SP:(j // 2 + 1) * SP]

    def queue_final(c, out=None):
        target = pend if out is None else out

        for grp in range(D // 512):
            def rtp(c=c, grp=grp):
                for j in range(4):
                    dk = grp * 4 + j
                    ep = j % 2
                    S.do("sync", lambda e, c=c, dk=dk, j=j: e.dma_start(
                        out=fslot(j), in_=rs_out[c][dk * 128:(dk + 1) * 128, :]),
                        wait=rget(f"rs{c}") + fget(f"ev{ep}"),
                        inc=sem(f"s_fev{ep}"), k=16)
                crb = sem("s_fev0").at() + sem("s_fev1").at()
                for tb in range(SP // 128):
                    bk = 6 + tb
                    for j in range(4):
                        src = fslot(j)[:, tb * 128:(tb + 1) * 128]
                        S.do("tensor", lambda e, j=j, bk=bk, src=src:
                             e.transpose(pb[bk][:, j * 128:(j + 1) * 128], src,
                                         id_f[:, :]),
                             wait=crb + fget(f"pbank{bk}"), inc=s_ftp)
                ctp = s_ftp.at()
                for tb in range(SP // 128):
                    bk = 6 + tb
                    S.do("scalar", lambda e, tb=tb, bk=bk: e.copy(
                        ev_b[tb][:, 0:512], pb[bk][:, 0:512]),
                        wait=ctp, inc=s_fvc)
                    FREE[f"pbank{bk}"] = s_fvc.at()
                    FREE[f"ev{tb}"] = [(sem(f"s_oacc{tb}").h,
                                        sem(f"s_oacc{tb}").v + 16)]
            target.append(rtp)

            def acc(c=c, grp=grp):
                for tb in range(SP // 128):
                    S.do("gpsimd", lambda e, c=c, grp=grp, tb=tb: e.dma_start(
                        out=y_out[c, tb * 128:(tb + 1) * 128,
                                  grp * 512:(grp + 1) * 512],
                        in_=ev_b[tb][:, 0:512], accum_op=OP.add),
                        wait=s_fvc.at() + RDY["prefill"],
                        inc=sem(f"s_oacc{tb}"), k=16)
            target.append(acc)

    emit_chunk_loads(0)
    for c in range(NCH):
        for b in range(B):
            emit_xnd_load(c, b)
            queue_ffn_sub(c, b, 0)
            emit_scan_b(c, b)
            if c + 1 < NCH:
                emit_chunk_loads(c + 1, only_b=b)
            fth = []
            if b == 1 and c >= 1:
                emit_rs(c - 1)
            p0 = len(pend)
            queue_down_sub(c, b, 0)
            for hf in range(1, NSUB):
                queue_ffn_sub(c, b, hf)
                queue_down_sub(c, b, hf)
            if fth:
                seg = pend[p0:]
                del pend[p0:]
                step = max(1, len(seg) // len(fth))
                si = 0
                for t in fth:
                    pend.extend(seg[si:si + step])
                    si += step
                    pend.append(t)
                pend.extend(seg[si:])
    pop_pend(len(pend))
    emit_rs(NCH - 1)

    for c in range(NCH):
        queue_final(c)
    pop_pend(len(pend))

    end_w = sem("s_oacc0").at() + sem("s_oacc1").at() + s_cp.at()
    for c in range(NCH):
        end_w = end_w + sem(f"s_ccrs{c}").at()
    S.do("sync", None, wait=end_w)
    S.do("gpsimd", None, wait=sem("s_oacc0").at() + sem("s_oacc1").at())

    with allow_ici_all_to_all():
        S.emit(nc)
    return nc


def shard_inputs(cfg, inp):
    D, N, R = cfg["D"], cfg["N"], cfg["R"]
    NC, NCH, TC, SP = cfg["NC"], cfg["NCH"], cfg["TC"], cfg["SP"]
    DOWN, DT, KT, IOWN, IT = (cfg["DOWN"], cfg["DT"], cfg["KT"], cfg["IOWN"],
                              cfg["IT"])
    GB = cfg["GB"]

    x = np.asarray(inp["L0_attn_x"], np.float32)
    gw = np.asarray(inp["L0_ffn_gate_w"], np.int32)
    uw = np.asarray(inp["L0_ffn_up_w"], np.int32)
    dw = np.asarray(inp["L0_ffn_down_w"], np.int32)
    gsc = np.asarray(inp["L0_ffn_gate_scale"], np.float32)
    usc = np.asarray(inp["L0_ffn_up_scale"], np.float32)
    dsc = np.asarray(inp["L0_ffn_down_scale"], np.float32)
    op = np.asarray(inp["L0_attn_out_proj_w"], np.float32)
    xp = np.asarray(inp["L0_attn_x_proj_w"], np.float32)
    dp = np.asarray(inp["L0_attn_dt_proj_w"], np.float32)
    dpb = np.asarray(inp["L0_attn_dt_proj_b"], np.float32)
    nw = np.asarray(inp["L0_attn_norm_w"], np.float32)
    nb = np.asarray(inp["L0_attn_norm_b"], np.float32)
    dpar = np.asarray(inp["L0_attn_D_param"], np.float32)
    alog = np.asarray(inp["L0_attn_A_log"], np.float32)

    def cols(v, nt):
        return np.ascontiguousarray(v.reshape(nt, 128).T)

    ident_bf = np.eye(128, dtype=BFNP)
    ident_f = np.eye(128, dtype=np.float32)
    xpT = np.ascontiguousarray(xp.T)
    dpT = np.ascontiguousarray(dp.T)
    dpb_c, nw_c, nb_c = cols(dpb, KT), cols(nw, KT), cols(nb, KT)

    maps = []
    for k in range(NC):
        b, s = k // GB, k % GB
        xb = x[b]
        stripes = [xb[c * TC + s * SP:c * TC + (s + 1) * SP] for c in range(NCH)]
        Dk = slice(k * DOWN, (k + 1) * DOWN)
        Ik = slice(k * IOWN, (k + 1) * IOWN)
        al = alog[Dk]
        al_t = np.zeros((128, DT * N), np.float32)
        for h in range(DT):
            al_t[:, h * N:(h + 1) * N] = al[h * 128:(h + 1) * 128, :]
        maps.append({
            "x_own": np.ascontiguousarray(np.concatenate(stripes, axis=0)),
            "wgT": np.ascontiguousarray(gw[Ik].T),
            "wuT": np.ascontiguousarray(uw[Ik].T),
            "wdT": np.ascontiguousarray(dw[:, Ik].T),
            "gs": cols(gsc[Ik], IT),
            "us": cols(usc[Ik], IT),
            "ds": np.ascontiguousarray(dsc.reshape(1, D)),
            "opT": np.ascontiguousarray(op[:, Dk].T),
            "xpT": xpT,
            "dpT": dpT,
            "dpb": dpb_c,
            "nw": nw_c,
            "nb": nb_c,
            "dpar": cols(dpar[Dk], DT),
            "alog": al_t,
            "ident_bf": ident_bf,
            "ident_f32": ident_f,
            "fconst": np.tile(np.array(
                [[1e-5, -0.5, -1.0, 1.0, -128.0, 0.0, 2.0, 0.5]], np.float32),
                (128, 1)),
        })
    return maps


def assemble_output(cfg, results):
    B, L, D = cfg["B"], cfg["L"], cfg["D"]
    NC, NCH, TC, SP, GB = cfg["NC"], cfg["NCH"], cfg["TC"], cfg["SP"], cfg["GB"]
    out = np.empty((B, L, D), np.float32)
    for k in range(NC):
        b, s = k // GB, k % GB
        yo = np.asarray(results[k]["y_out"], np.float32).reshape(NCH, SP, D)
        for c in range(NCH):
            out[b, c * TC + s * SP:c * TC + (s + 1) * SP] = yo[c]
    return out


def kernel(**inputs):
    cfg = CFG
    nc = build_program(cfg)
    in_maps = shard_inputs(cfg, inputs)
    res = run_bass_kernel_spmd(nc, in_maps, core_ids=list(range(cfg["NC"])))
    return assemble_output(cfg, res.results)


# version 20
# speedup vs baseline: 1.1747x; 1.0892x over previous
import contextlib
import numpy as np
import ml_dtypes
import concourse.bass as bass
import concourse.mybir as mybir
from concourse.bass_utils import run_bass_kernel_spmd
from concourse.aot_env import allow_ici_all_to_all

F32 = mybir.dt.float32
BF16 = mybir.dt.bfloat16
I32 = mybir.dt.int32
AF = mybir.ActivationFunctionType
OP = mybir.AluOpType
BFNP = ml_dtypes.bfloat16


def make_cfg(D=2048, L=4096, B=2, N=64, R=128, CHUNK=512, I=8192, NC=8):
    c = dict(D=D, L=L, B=B, N=N, R=R, CHUNK=CHUNK, I=I, NC=NC)
    c["GB"] = NC // B
    c["NCH"] = 4
    c["TC"] = L // c["NCH"]
    c["SP"] = c["TC"] // c["GB"]
    c["TCORE"] = c["NCH"] * c["SP"]
    assert c["TCORE"] == B * L // NC
    assert c["SP"] % 128 == 0
    c["DOWN"] = D // NC
    c["DT"] = c["DOWN"] // 128
    c["NT"] = c["DT"] * B
    c["KT"] = D // 128
    c["IOWN"] = I // NC
    c["IT"] = c["IOWN"] // 128
    c["HT"] = D // 128
    c["TS"] = min(512, c["TC"])
    c["NSUB"] = c["TC"] // c["TS"]
    c["CHW"] = min(CHUNK, c["TC"])
    c["REP"] = c["TC"] // c["CHW"]
    c["YC"] = min(512, c["TC"])
    c["YH"] = c["TC"] // c["YC"]
    c["NPROJ"] = R + 2 * N
    c["TPT"] = c["TCORE"] // 128
    c["NSEQ"] = min(512, c["TCORE"])
    c["NA"] = c["TCORE"] // c["NSEQ"]
    assert c["DT"] * c["YH"] * 2 <= 4 + 4
    assert c["DT"] >= 1 and c["IT"] >= 1
    return c


CFG = make_cfg()


class Sem:
    __slots__ = ("h", "v")

    def __init__(self, handle):
        self.h = handle
        self.v = 0

    def at(self):
        return [(self.h, self.v)]


class Sched:
    def __init__(self):
        self.items = []

    def do(self, eng, fn, wait=(), inc=None, k=1):
        byid = {}
        for (h, v) in wait:
            if v > 0 and (id(h) not in byid or byid[id(h)][1] < v):
                byid[id(h)] = (h, v)
        waits = list(byid.values())
        incs = None
        if inc is not None:
            inc.v += k
            incs = (inc.h, k)
        self.items.append((eng, fn, waits, incs))

    def emit(self, nc):
        with nc.Block() as block:
            for name in ("sync", "scalar", "vector", "tensor", "gpsimd"):
                items = [it for it in self.items if it[0] == name]
                if not items:
                    continue

                def body(eng, items=items):
                    hi = {}
                    for _, fn, waits, incs in items:
                        for (h, v) in waits:
                            if hi.get(id(h), 0) < v:
                                eng.wait_ge(h, v)
                                hi[id(h)] = v
                        inst = fn(eng) if fn is not None else None
                        if incs is not None:
                            assert inst is not None
                            inst.then_inc(incs[0], incs[1])

                getattr(block, name)(body)


def build_program(cfg):
    D, L, B, N, R = cfg["D"], cfg["L"], cfg["B"], cfg["N"], cfg["R"]
    I, NC = cfg["I"], cfg["NC"]
    GB, NCH, TC, SP, TCORE = cfg["GB"], cfg["NCH"], cfg["TC"], cfg["SP"], cfg["TCORE"]
    DOWN, DT, NT, KT = cfg["DOWN"], cfg["DT"], cfg["NT"], cfg["KT"]
    IOWN, IT, HT, TS, NSUB = cfg["IOWN"], cfg["IT"], cfg["HT"], cfg["TS"], cfg["NSUB"]
    REP, CHW, YC, YH, NPROJ = (cfg["REP"], cfg["CHW"], cfg["YC"], cfg["YH"],
                               cfg["NPROJ"])
    TPT, NSEQ, NA = cfg["TPT"], cfg["NSEQ"], cfg["NA"]

    nc = bass.Bass("TRN2", target_bir_lowering=False)
    es = nc.ctx

    def din(name, shape, dtype=F32):
        return nc.dram_tensor(name, list(shape), dtype, kind="ExternalInput")

    x_own = din("x_own", [TCORE, D])
    wgT = din("wgT", [D, IOWN], I32)
    wuT = din("wuT", [D, IOWN], I32)
    wdT = din("wdT", [IOWN, D], I32)
    gs_in = din("gs", [128, IT])
    us_in = din("us", [128, IT])
    ds_in = din("ds", [1, D])
    opT = din("opT", [DOWN, D])
    xpT = din("xpT", [D, NPROJ])
    dpT = din("dpT", [R, D])
    dpb_in = din("dpb", [128, KT])
    nw_in = din("nw", [128, KT])
    nb_in = din("nb", [128, KT])
    dpar_in = din("dpar", [128, DT])
    alog_in = din("alog", [128, DT * N])
    id_bf_in = din("ident_bf", [128, 128], BF16)
    id_f_in = din("ident_f32", [128, 128], F32)
    fc_in = din("fconst", [128, 8], F32)

    y_out = nc.dram_tensor("y_out", [NCH, SP, D], F32, kind="ExternalOutput")

    xn_ag_in = nc.dram_tensor("xn_ag_in", [D, TCORE], BF16)
    xn_ag_out = nc.dram_tensor("xn_ag_out", [NC * D, TCORE], BF16)
    bc_ag_in = nc.dram_tensor("bc_ag_in", [2 * N, TCORE], BF16)
    bc_ag_out = nc.dram_tensor("bc_ag_out", [NC * 2 * N, TCORE], BF16)
    dt_a2a_in = nc.dram_tensor("dt_a2a_in", [D, TCORE], BF16)
    dt_a2a_out = nc.dram_tensor("dt_a2a_out", [D, TCORE], BF16)
    dx_a2a_in = nc.dram_tensor("dx_a2a_in", [D, TCORE], BF16)
    dx_a2a_out = nc.dram_tensor("dx_a2a_out", [D, TCORE], BF16)
    xn_a2a_out = nc.dram_tensor("xn_a2a_out", [D, TCORE], BF16)
    rs_in = [nc.dram_tensor(f"rs_in_{c}", [NC * D, SP], F32) for c in range(NCH)]
    rs_out = [nc.dram_tensor(f"rs_out_{c}", [D, SP], F32) for c in range(NCH)]

    def sb(name, shape, dtype):
        return es.enter_context(nc.sbuf_tensor(name, list(shape), dtype))

    a_t = sb("a_t", [128, DT * N], F32)
    a_scr = sb("a_scr", [128, DT * N], F32)
    gs_t = sb("gs_t", [128, IT], F32)
    us_t = sb("us_t", [128, IT], F32)
    opT_t = [sb(f"opT{i}", [128, D], BF16) for i in range(DT)]
    dpb_t = sb("dpb_t", [128, KT], F32)
    nw_t = sb("nw_t", [128, KT], F32)
    nb_t = sb("nb_t", [128, KT], F32)
    dpar_t = sb("dpar_t", [128, DT], F32)
    id_bf = sb("id_bf", [128, 128], BF16)
    id_f = sb("id_f", [128, 128], F32)
    fc_t = sb("fc_t", [128, 8], F32)
    FCONST = [1e-5, -0.5, -1.0, 1.0, -128.0, 0.0, 2.0, 0.5]
    for fi, fv in enumerate(FCONST):
        nc.const_aps.register(F32, fv, fc_t[:, fi:fi + 1])

    pb = [es.enter_context(nc.psum_tensor(f"pb{i}", [128, 512], F32))
          for i in range(8)]
    pb_bf = [p[:, :].bitcast(BF16) for p in pb]

    S = Sched()
    _sems = {}

    def sem(name):
        if name not in _sems:
            _sems[name] = Sem(es.enter_context(nc.semaphore(name)))
        return _sems[name]

    FREE = {}
    RDY = {}

    def fget(nm):
        return FREE.get(nm, [])

    def rget(nm):
        r = RDY.get(nm)
        assert r is not None, f"missing RDY[{nm}]"
        return r

    s_cd = sem("s_cd")
    s_cc0 = sem("s_cc0")
    for (dst, src) in [(gs_t, gs_in), (us_t, us_in), (dpb_t, dpb_in),
                       (nw_t, nw_in), (nb_t, nb_in), (dpar_t, dpar_in),
                       (id_bf, id_bf_in), (id_f, id_f_in), (a_scr, alog_in),
                       (fc_t, fc_in)]:
        S.do("sync", lambda e, d=dst, s=src: e.dma_start(out=d[:, :], in_=s[:, :]),
             inc=s_cd, k=16)
    c_cd = s_cd.at()
    S.do("scalar", lambda e: e.activation(a_t[:, :], a_scr[:, :], AF.Exp),
         wait=c_cd, inc=s_cc0)
    S.do("vector", lambda e: e.tensor_scalar_mul(a_scr[:, :], a_t[:, :], -1.0),
         wait=s_cc0.at(), inc=s_cc0)
    RDY["a_t"] = s_cc0.at()
    RDY["consts"] = c_cd

    esA = es.enter_context(contextlib.ExitStack())

    def sbA(name, shape, dtype):
        return esA.enter_context(nc.sbuf_tensor(name, list(shape), dtype))

    wsm = sbA("wsm", [128, D], F32)
    xpT_t = sbA("xpT_t", [128, KT * NPROJ], BF16)
    dpT_t = sbA("dpT_t", [128, D], BF16)
    xa_xb = [sbA(f"xa_x{p}", [128, D], F32) for p in range(2)]
    xa_st = sbA("xa_st", [128, D], BF16)
    xa_st2 = sbA("xa_st2", [128, D], BF16)
    xa_tr = sbA("xa_tr", [128, 1024], BF16)
    NST = max(1, D // 512)
    xa_stat = sbA("xa_stat", [128, 6 * NST + 6], F32)
    xa_dr = sbA("xa_dr", [128, TCORE], BF16)
    xa_bc = sbA("xa_bc", [128, TCORE], BF16)
    xa_xn2 = [sbA(f"xa_xn2_{k}", [128, TCORE], BF16) for k in range(KT)]
    xa_dtT = [sbA(f"xa_dtT_{k}", [128, TCORE], BF16) for k in range(KT)]
    xa_sp = [sbA(f"xa_sp{p}", [128, NSEQ], F32) for p in range(2)]

    s_sw = sem("s_sw")
    s_swc = sem("s_swc")
    for i in range(DT):
        S.do("sync", lambda e, i=i: e.dma_start(
            out=wsm[:, 0:D], in_=opT[i * 128:(i + 1) * 128, :]),
            wait=fget("wsm"), inc=s_sw, k=16)
        S.do("scalar", lambda e, i=i: e.copy(opT_t[i][:, :], wsm[:, 0:D]),
             wait=s_sw.at(), inc=s_swc)
        FREE["wsm"] = s_swc.at()
    for k in range(KT):
        S.do("sync", lambda e, k=k: e.dma_start(
            out=wsm[:, 0:NPROJ], in_=xpT[k * 128:(k + 1) * 128, :]),
            wait=fget("wsm"), inc=s_sw, k=16)
        S.do("scalar", lambda e, k=k: e.copy(
            xpT_t[:, k * NPROJ:(k + 1) * NPROJ], wsm[:, 0:NPROJ]),
            wait=s_sw.at(), inc=s_swc)
        FREE["wsm"] = s_swc.at()
    S.do("sync", lambda e: e.dma_start(out=wsm[:, 0:D], in_=dpT[:, :]),
         wait=fget("wsm"), inc=s_sw, k=16)
    S.do("scalar", lambda e: e.copy(dpT_t[:, :], wsm[:, 0:D]),
         wait=s_sw.at(), inc=s_swc)
    RDY["smallw"] = s_swc.at()

    s_xd = sem("s_xd")
    s_ln = sem("s_ln")
    s_tp = sem("s_tp")
    s_tpe = sem("s_tpe")
    s_agw = sem("s_agw")
    s_cc = sem("s_cc")
    TRT = max(1, D // 1024)

    for ti in range(TPT):
        pX = ti % 2
        wq = min(512, D)
        cxq = []
        cx = []
        for qj in range(NST):
            sq = sem(f"s_xdq{qj}")
            S.do("sync", lambda e, ti=ti, pX=pX, qj=qj, wq=wq: e.dma_start(
                out=xa_xb[pX][:, qj * wq:(qj + 1) * wq],
                in_=x_own[ti * 128:(ti + 1) * 128, qj * wq:(qj + 1) * wq]),
                wait=fget(f"xax{pX}"), inc=sq, k=16)
            cxq.append(sq.at())
            cx = cx + sq.at()

        def ln_pass(src_ap, dst, wait0, wfree, wq0=None):
            st6 = xa_stat[:, 0:6 * NST]
            for j in range(NST):
                w = min(512, D)
                S.do("vector", lambda e, j=j, s=src_ap, w=w: e.bn_stats(
                    st6[:, 6 * j:6 * (j + 1)], s[:, w * j:w * (j + 1)]),
                    wait=(wq0[j] if wq0 is not None else wait0), inc=s_ln)
            mv = xa_stat[:, 6 * NST:6 * NST + 2]
            S.do("vector", lambda e, mv=mv: e.bn_aggr(mv, st6),
                 wait=s_ln.at(), inc=s_ln)
            lnv = xa_stat[:, 6 * NST + 4:6 * NST + 5]
            rstd = xa_stat[:, 6 * NST + 2:6 * NST + 3]
            nmr = xa_stat[:, 6 * NST + 3:6 * NST + 4]
            S.do("scalar", lambda e, mv=mv, lv=lnv: e.activation(
                lv, mv[:, 1:2], AF.Ln, bias=1e-5),
                wait=s_ln.at() + RDY["consts"], inc=s_ln)
            S.do("scalar", lambda e, lv=lnv, r=rstd: e.activation(
                r, lv, AF.Exp, scale=-0.5), wait=s_ln.at(), inc=s_ln)
            S.do("vector", lambda e, mv=mv, r=rstd, nm=nmr:
                 e.scalar_tensor_tensor(nm, mv[:, 0:1], -1.0, r, OP.mult,
                                        OP.mult),
                 wait=s_ln.at(), inc=s_ln)
            S.do("scalar", lambda e, s=src_ap, d=dst, r=rstd, nm=nmr:
                 e.activation(d[:, :], s[:, :], AF.Identity, scale=r, bias=nm),
                 wait=s_ln.at() + wfree, inc=s_ln)
            return s_ln.at()

        c_xn = ln_pass(xa_xb[pX], xa_st, cx, fget("xa_st"), wq0=cxq)
        FREE[f"xax{pX}"] = c_xn
        c_xnn = ln_pass(xa_st, xa_st2, c_xn, fget("xa_st2"))

        for (src, kind, cready) in ((xa_st, "xn", c_xn), (xa_st2, "xnn", c_xnn)):
            for half in range(TRT):
                bk = 4 + (half % 2)
                bank = pb_bf[bk]
                nj = min(8, D // 128)
                for j in range(nj):
                    kk = half * 8 + j
                    S.do("tensor", lambda e, s=src, kk=kk, j=j, bank=bank:
                         e.transpose(bank[:, j * 128:(j + 1) * 128],
                                     s[:, kk * 128:(kk + 1) * 128], id_bf[:, :]),
                         wait=cready + RDY["consts"] + fget(f"pbank{bk}"),
                         inc=s_tp)
                c_t = s_tp.at()
                if kind == "xn":
                    S.do("scalar", lambda e, bank=bank, nj=nj: e.copy(
                        xa_tr[:, 0:nj * 128], bank[:, 0:nj * 128]),
                        wait=c_t + fget("xa_tr"), inc=s_tpe)
                    FREE[f"pbank{bk}"] = s_tpe.at()
                    S.do("sync", lambda e, ti=ti, half=half, nj=nj: e.dma_start(
                        out=xn_ag_in.ap().rearrange("(a p) t -> p a t", p=128)
                        [:, half * 8:half * 8 + nj, ti * 128:(ti + 1) * 128],
                        in_=xa_tr[:, 0:nj * 128].rearrange(
                            "p (a t) -> p a t", a=nj)),
                        wait=s_tpe.at(), inc=s_agw, k=16)
                    FREE["xa_tr"] = s_agw.at()
                else:
                    for j in range(nj):
                        kk = half * 8 + j
                        S.do("scalar", lambda e, kk=kk, j=j, bank=bank, ti=ti:
                             e.activation(
                                 xa_xn2[kk][:, ti * 128:(ti + 1) * 128],
                                 bank[:, j * 128:(j + 1) * 128], AF.Identity,
                                 scale=nw_t[:, kk:kk + 1],
                                 bias=nb_t[:, kk:kk + 1]),
                             wait=c_t + RDY["consts"], inc=s_tpe)
                    FREE[f"pbank{bk}"] = s_tpe.at()
            if kind == "xn":
                FREE["xa_st"] = s_tp.at() + c_xnn
            else:
                FREE["xa_st2"] = s_tp.at()
    c_xn2 = s_tpe.at()

    s_ccxn = sem("s_ccxn")
    s_ccxa = sem("s_ccxa")
    S.do("gpsimd", lambda e: e.collective_compute(
        "AllGather", OP.bypass, replica_groups=[list(range(NC))],
        ins=[xn_ag_in.ap().opt()], outs=[xn_ag_out.ap().opt()]),
        wait=s_agw.at(), inc=s_ccxn, k=1)
    RDY["xn_ag"] = s_ccxn.at()
    S.do("gpsimd", lambda e: e.collective_compute(
        "AllToAll", OP.bypass, replica_groups=[list(range(NC))],
        ins=[xn_ag_in.ap().opt()], outs=[xn_a2a_out.ap().opt()]),
        wait=s_agw.at(), inc=s_ccxa, k=1)
    RDY["xn_a2a"] = s_ccxa.at()

    s_mm = sem("s_mm")
    s_ev = sem("s_ev")
    s_dw = sem("s_dw")
    for na in range(NA):
        for m in range(NPROJ // 128):
            bank = pb[m]
            for k in range(KT):
                w0 = (c_xn2 + RDY["smallw"] + fget(f"pbank{m}")) if k == 0 else []
                S.do("tensor", lambda e, na=na, m=m, k=k, bank=bank: e.matmul(
                    bank[:, 0:NSEQ],
                    xpT_t[:, k * NPROJ + m * 128:k * NPROJ + (m + 1) * 128],
                    xa_xn2[k][:, na * NSEQ:(na + 1) * NSEQ],
                    start=(k == 0), stop=(k == KT - 1)),
                    wait=w0, inc=s_mm)
        S.do("scalar", lambda e, na=na: e.copy(
            xa_dr[:, na * NSEQ:(na + 1) * NSEQ], pb[0][:, 0:NSEQ]),
            wait=s_mm.at(), inc=s_ev)
        S.do("scalar", lambda e, na=na: e.copy(
            xa_bc[:, na * NSEQ:(na + 1) * NSEQ], pb[1][:, 0:NSEQ]),
            wait=s_mm.at(), inc=s_ev)
        FREE["pbank0"] = s_ev.at()
        FREE["pbank1"] = s_ev.at()
    c_proj = s_ev.at()
    s_bcw = sem("s_bcw")
    s_ccbc = sem("s_ccbc")
    S.do("sync", lambda e: e.dma_start(out=bc_ag_in.ap(), in_=xa_bc[0:2 * N, :]),
         wait=c_proj, inc=s_bcw, k=16)
    S.do("gpsimd", lambda e: e.collective_compute(
        "AllGather", OP.bypass, replica_groups=[list(range(NC))],
        ins=[bc_ag_in.ap().opt()], outs=[bc_ag_out.ap().opt()]),
        wait=s_bcw.at(), inc=s_ccbc, k=1)
    RDY["bc_ag"] = s_ccbc.at()

    s_dx = sem("s_dx")
    zb, rl = xa_sp[0], xa_sp[1]
    a1 = wsm[:, 0:NSEQ]
    a2 = wsm[:, NSEQ:2 * NSEQ]
    for m in range(KT):
        for na in range(NA):
            bk = 4 + ((m * NA + na) % 2)
            bank = pb[bk]
            S.do("tensor", lambda e, m=m, na=na, bank=bank: e.matmul(
                bank[:, 0:NSEQ], dpT_t[:, m * 128:(m + 1) * 128],
                xa_dr[:, na * NSEQ:(na + 1) * NSEQ], start=True, stop=True),
                wait=c_proj + RDY["smallw"] + fget(f"pbank{bk}"), inc=s_mm)
            S.do("scalar", lambda e, m=m, bank=bank: e.activation(
                zb[:, :], bank[:, 0:NSEQ], AF.Identity,
                bias=dpb_t[:, m:m + 1]),
                wait=s_mm.at() + fget("sp"), inc=s_ev)
            FREE[f"pbank{bk}"] = s_ev.at()
            S.do("scalar", lambda e: e.activation(a1[:, :], zb[:, :], AF.Abs),
                 wait=s_ev.at() + fget("wsm"), inc=s_ev)
            S.do("scalar", lambda e: e.activation(
                a2[:, :], a1[:, :], AF.Exp, scale=-1.0),
                wait=s_ev.at() + RDY["consts"], inc=s_ev)
            S.do("scalar", lambda e: e.activation(
                a1[:, :], a2[:, :], AF.Ln, bias=1.0), wait=s_ev.at(), inc=s_ev)
            S.do("scalar", lambda e: e.activation(rl[:, :], zb[:, :], AF.Relu),
                 wait=s_ev.at(), inc=s_ev)
            S.do("vector", lambda e, m=m, na=na: e.tensor_add(
                xa_dtT[m][:, na * NSEQ:(na + 1) * NSEQ], rl[:, :], a1[:, :]),
                wait=s_ev.at(), inc=s_ev)
            FREE["sp"] = s_ev.at()
            FREE["wsm"] = s_ev.at()
    c_dt = s_ev.at()
    s_dtw = sem("s_dtw")
    s_dxw = sem("s_dxw")
    s_ccdt = sem("s_ccdt")
    s_ccdx = sem("s_ccdx")
    for m in range(KT):
        S.do("sync", lambda e, m=m: e.dma_start(
            out=dt_a2a_in[m * 128:(m + 1) * 128, :], in_=xa_dtT[m][:, :]),
            wait=c_dt, inc=s_dtw, k=16)
        S.do("vector", lambda e, m=m: e.tensor_mul(
            xa_dr[:, :], xa_dtT[m][:, :], xa_xn2[m][:, :]),
            wait=c_dt + fget("xa_dr"), inc=s_dx)
        S.do("sync", lambda e, m=m: e.dma_start(
            out=dx_a2a_in[m * 128:(m + 1) * 128, :], in_=xa_dr[:, :]),
            wait=s_dx.at(), inc=s_dxw, k=16)
        FREE["xa_dr"] = s_dxw.at()
    c_a2aw = s_dtw.at() + s_dxw.at()
    S.do("gpsimd", lambda e: e.collective_compute(
        "AllToAll", OP.bypass, replica_groups=[list(range(NC))],
        ins=[dt_a2a_in.ap().opt()], outs=[dt_a2a_out.ap().opt()]),
        wait=c_a2aw, inc=s_ccdt, k=1)
    RDY["dt_a2a"] = s_ccdt.at()
    S.do("gpsimd", lambda e: e.collective_compute(
        "AllToAll", OP.bypass, replica_groups=[list(range(NC))],
        ins=[dx_a2a_in.ap().opt()], outs=[dx_a2a_out.ap().opt()]),
        wait=c_a2aw, inc=s_ccdx, k=1)
    RDY["dx_a2a"] = s_ccdx.at()

    esA.close()

    dsb = sb("dsb", [128, D], BF16)
    dt_ch = [sb(f"dt_ch{i}", [128, TC], BF16) for i in range(NT)]
    dx_ch = [sb(f"dx_ch{i}", [128, TC], BF16) for i in range(NT)]
    NDA = 5
    da_b = [sb(f"da{i}", [128, TC], BF16) for i in range(NDA)]
    u_b = [sb("u0", [128, TC], BF16)] * 2
    h_b = [sb(f"h{i}", [128, TC], BF16) for i in range(2)]
    z_b = [sb(f"z{i}", [128, TC], BF16) for i in range(2)]
    NBB = 4
    bb_b = [sb(f"bb{i}", [128, TC], BF16) for i in range(NBB)]
    xnd_t = [sb(f"xnd{i}", [128, TC], BF16) for i in range(2)]
    cb_b = [sb(f"cb{i}", [128, CHW], BF16) for i in range(NBB)]
    carry = [sb(f"carry{i}", [128, N], BF16) for i in range(NT)]
    rhs_t = [sb(f"rhs{i}", [128, TC], BF16) for i in range(NT)]
    xns = [sb(f"xns_{k}", [128, TS], BF16) for k in range(KT)]
    hsub_a = sb("hsub_a", [128, max(IT * TS, 4 * 1024)], BF16)
    hsub = [hsub_a[:, i * TS:(i + 1) * TS] for i in range(IT)]
    fin32 = hsub_a[:, :].bitcast(F32)
    ffin = [fin32[:, tb * 512:(tb + 1) * 512] for tb in range(2)]
    xf_g = [fin32[:, (2 + tb) * 512:(3 + tb) * 512] for tb in range(2)]
    sg_b = [sb(f"sg{p}", [128, TS], BF16) for p in range(2)]
    usb_b = [sb(f"usb{p}", [128, TS], BF16) for p in range(2)]
    ev_b = [sb(f"ev{i}", [128, 512], F32) for i in range(2)]

    warena = sb("warena", [128, 2 * KT * IOWN + IT * D], BF16)
    wg_t = [warena[:, k * IOWN:(k + 1) * IOWN] for k in range(KT)]
    wu_t = [warena[:, (KT + k) * IOWN:(KT + k + 1) * IOWN] for k in range(KT)]
    wd_t = [warena[:, 2 * KT * IOWN + k * D:2 * KT * IOWN + (k + 1) * D]
            for k in range(IT)]
    WSW = min(512, max(IOWN, D))
    wstage = [sb(f"wstage{p}", [128, WSW], I32) for p in range(2)]

    s_dsb = sem("s_dsb")
    S.do("gpsimd", lambda e: e.dma_start(
        out=dsb[:, :], in_=ds_in.ap().partition_broadcast(128)),
        wait=c_a2aw, inc=s_dsb, k=16)
    RDY["dsb"] = s_dsb.at()

    s_wd = [sem("s_wd0"), sem("s_wd1")]
    s_wc = sem("s_wc")
    s_wm = sem("s_wm")
    n_w = 0
    for (src, tiles, nt_, is_down, width) in (
            (wgT, wg_t, KT, False, IOWN), (wuT, wu_t, KT, False, IOWN),
            (wdT, wd_t, IT, True, D)):
        for k in range(nt_):
            for h0 in range(0, width, WSW):
                w = min(WSW, width - h0)
                p = n_w % 2
                w0 = fget(f"wstage{p}")
                if n_w == 0:
                    w0 = w0 + c_a2aw + s_dx.at()
                S.do("sync", lambda e, src=src, k=k, p=p, h0=h0, w=w:
                     e.dma_start(out=wstage[p][:, 0:w],
                                 in_=src[k * 128:(k + 1) * 128, h0:h0 + w]),
                     wait=w0, inc=s_wd[p], k=16)
                if not is_down:
                    S.do("scalar", lambda e, tiles=tiles, k=k, p=p, h0=h0, w=w:
                         e.activation(tiles[k][:, h0:h0 + w],
                                      wstage[p][:, 0:w], AF.Copy, bias=-128.0),
                         wait=s_wd[p].at() + RDY["consts"], inc=s_wc)
                    FREE[f"wstage{p}"] = s_wc.at()
                else:
                    for t0 in range(0, w, TS):
                        tw = min(TS, w - t0)
                        pp = (h0 // WSW + t0 // TS) % 2
                        S.do("scalar", lambda e, p=p, t0=t0, tw=tw, pp=pp:
                             e.activation(usb_b[pp][:, 0:tw],
                                          wstage[p][:, t0:t0 + tw], AF.Copy,
                                          bias=-128.0),
                             wait=s_wd[p].at() + fget(f"usb{pp}") +
                             RDY["consts"], inc=s_wc)
                        S.do("vector", lambda e, tiles=tiles, k=k, h0=h0,
                             t0=t0, tw=tw, pp=pp: e.tensor_mul(
                                 tiles[k][:, h0 + t0:h0 + t0 + tw],
                                 usb_b[pp][:, 0:tw],
                                 dsb[:, h0 + t0:h0 + t0 + tw]),
                             wait=s_wc.at() + RDY["dsb"], inc=s_wm)
                        FREE[f"usb{pp}"] = s_wm.at()
                    FREE[f"wstage{p}"] = s_wc.at()
                n_w += 1
    RDY["weights"] = s_wc.at() + s_wm.at()

    s_sc = sem("s_sc")
    nper = CHW // SP
    bc_v = bc_ag_out.ap().rearrange("(b s r) t -> b r s t", b=B, s=GB)

    s_bb = sem("s_bb")
    s_da = sem("s_da")
    s_u = sem("s_u")
    s_h = sem("s_h")
    s_z = sem("s_z")
    s_ymm = sem("s_ymm")
    s_yev = sem("s_yev")
    s_rhs = sem("s_rhs")
    s_cp = sem("s_cp")
    s_xns = sem("s_xns")
    s_gu = sem("s_gu")
    s_sil = sem("s_sil")
    s_hs = sem("s_hs")
    s_dn = sem("s_dn")
    s_oev = sem("s_oev")
    s_orw = sem("s_orw")
    s_rs = sem("s_rs")
    s_fd = sem("s_fd")
    s_ftp = sem("s_ftp")
    s_fa = sem("s_fa")
    s_out = sem("s_out")

    it_scan = [0]
    pend = []

    def pop_pend(k):
        for _ in range(min(k, len(pend))):
            pend.pop(0)()

    def emit_chunk_loads(c, only_b=None):
        for i in range(NT):
            b, half = i // DT, i % DT
            if only_b is not None and b != only_b:
                continue

            def mk(dst, srcd, b=b, half=half, c=c):
                def f(e):
                    a = srcd.ap().rearrange("(j dd) t -> j dd t", j=NC)
                    a = a.rearrange("(b s) dd t -> b s dd t", b=B)
                    a = a[b, :, half * 128:(half + 1) * 128,
                          c * SP:(c + 1) * SP]
                    a = a.transpose([1, 0, 2])
                    return e.dma_start(
                        out=dst[:, :].rearrange("p (s t) -> p s t", s=GB),
                        in_=a)
                return f

            sdt = sem(f"s_dtch{i}")
            sdx = sem(f"s_dxch{i}")
            S.do("sync", mk(dt_ch[i], dt_a2a_out),
                 wait=RDY["dt_a2a"] + fget(f"dt_ch{i}"), inc=sdt, k=16)
            RDY[f"dt_ch{i}"] = sdt.at()
            S.do("sync", mk(dx_ch[i], dx_a2a_out),
                 wait=RDY["dx_a2a"] + fget(f"dx_ch{i}"), inc=sdx, k=16)
            RDY[f"dx_ch{i}"] = sdx.at()


    def queue_ffn_sub(c, b, hf):
        for k in range(KT):
            def ldx(k=k, b=b, c=c, hf=hf):
                def mkx(e):
                    a = xn_ag_out.ap().rearrange("(j d) t -> j d t", j=NC)
                    a = a.rearrange("(b s) d t -> b s d t", b=B)
                    npieces = TS // SP
                    s0 = hf * npieces
                    a = a[b, s0:s0 + npieces, k * 128:(k + 1) * 128,
                          c * SP:(c + 1) * SP]
                    a = a.transpose([1, 0, 2])
                    return e.dma_start(
                        out=xns[k][:, :].rearrange("p (s t) -> p s t",
                                                   s=npieces), in_=a)
                S.do("sync", mkx, wait=RDY["xn_ag"] + fget(f"xns{k}"),
                     inc=s_xns, k=16)
                RDY[f"xns{k}"] = s_xns.at()
            pend.append(ldx)

        def mk_sil(it):
            def sil(it=it):
                pp = it % 2
                bg = 2 + 2 * (it % 2)
                bu = 3 + 2 * (it % 2)
                S.do("scalar", lambda e, it=it, pp=pp, bg=bg: e.activation(
                    sg_b[pp][:, :], pb[bg][:, 0:TS], AF.Silu,
                    scale=gs_t[:, it:it + 1]),
                    wait=rget(f"gu{bg}") + RDY["consts"] + fget(f"sg{pp}"),
                    inc=s_sil)
                S.do("scalar", lambda e, it=it, pp=pp, bu=bu: e.activation(
                    usb_b[pp][:, :], pb[bu][:, 0:TS], AF.Copy,
                    scale=us_t[:, it:it + 1]),
                    wait=rget(f"gu{bu}") + s_sil.at() + fget(f"usb{pp}"),
                    inc=s_sil)
                FREE[f"pbank{bg}"] = s_sil.at()
                FREE[f"pbank{bu}"] = s_sil.at()
                S.do("vector", lambda e, it=it, pp=pp: e.tensor_mul(
                    hsub[it][:, :], sg_b[pp][:, :], usb_b[pp][:, :]),
                    wait=s_sil.at() + fget(f"hsub{it}"), inc=s_hs)
                FREE[f"sg{pp}"] = s_hs.at()
                FREE[f"usb{pp}"] = s_hs.at()
                RDY[f"hsub{it}"] = s_hs.at()
            return sil

        for it in range(IT):
            for (boff, wt) in ((2, wg_t), (3, wu_t)):
                bk = boff + 2 * (it % 2)
                for k in range(KT):
                    def gmm(it=it, bk=bk, wt=wt, k=k):
                        w0 = []
                        if k == 0:
                            w0 = RDY["weights"] + fget(f"pbank{bk}")
                            for kk2 in range(KT):
                                w0 = w0 + rget(f"xns{kk2}")
                        S.do("tensor", lambda e, it=it, k=k, wt=wt, bk=bk:
                             e.matmul(pb[bk][:, 0:TS],
                                      wt[k][:, it * 128:(it + 1) * 128],
                                      xns[k][:, :], start=(k == 0),
                                      stop=(k == KT - 1)),
                             wait=w0, inc=s_gu)
                        if k == KT - 1:
                            RDY[f"gu{bk}"] = s_gu.at()
                    pend.append(gmm)
            if it >= 1:
                pend.append(mk_sil(it - 1))
            if it == IT - 1:
                pend.append(mk_sil(it))

        def relx():
            for kk2 in range(KT):
                FREE[f"xns{kk2}"] = s_gu.at()
        pend.append(relx)

    def queue_down_sub(c, b, hf):
        t0 = hf * TS
        npieces = TS // SP

        def mk_evac(hm):
            def ev(hm=hm):
                bk = 6 + (hm % 2)
                ep = hm % 2
                S.do("scalar", lambda e, bk=bk, ep=ep: e.copy(
                    ev_b[ep][:, 0:TS], pb[bk][:, 0:TS]),
                    wait=rget(f"dn{bk}") + fget(f"ev{ep}"), inc=s_oev)
                FREE[f"pbank{bk}"] = s_oev.at()

                def wrs(e, c=c, b=b, hf=hf, hm=hm, ep=ep, npieces=npieces):
                    a = rs_in[c].ap().rearrange("(j d) t -> j d t", j=NC)
                    a = a.rearrange("(b s) d t -> b s d t", b=B)
                    s0 = hf * npieces
                    a = a[b, s0:s0 + npieces, hm * 128:(hm + 1) * 128, :]
                    a = a.transpose([1, 0, 2])
                    return e.dma_start(
                        out=a, in_=ev_b[ep][:, 0:TS].rearrange(
                            "p (s t) -> p s t", s=npieces))

                S.do("sync", wrs, wait=s_oev.at(), inc=sem(f"s_rsw{ep}"), k=16)
                FREE[f"ev{ep}"] = sem(f"s_rsw{ep}").at()
            return ev

        for hm in range(HT):
            bk = 6 + (hm % 2)
            for kk in range(IT):
                def dmm(hm=hm, kk=kk, bk=bk):
                    w0 = []
                    if kk == 0:
                        w0 = RDY["weights"] + fget(f"pbank{bk}")
                        for it2 in range(IT):
                            w0 = w0 + rget(f"hsub{it2}")
                    S.do("tensor", lambda e, hm=hm, kk=kk, bk=bk: e.matmul(
                        pb[bk][:, 0:TS], wd_t[kk][:, hm * 128:(hm + 1) * 128],
                        hsub[kk][:, :], start=(kk == 0), stop=False),
                        wait=w0, inc=s_dn)
                pend.append(dmm)
            for ti in range(DT):
                def omm(hm=hm, ti=ti, bk=bk, b=b, t0=t0):
                    i = b * DT + ti
                    w0 = []
                    if ti == 0:
                        w0 = rget(f"rhs{b * DT}") + rget(f"rhs{b * DT + DT - 1}")
                    S.do("tensor", lambda e, hm=hm, ti=ti, i=i, t0=t0, bk=bk:
                         e.matmul(pb[bk][:, 0:TS],
                                  opT_t[ti][:, hm * 128:(hm + 1) * 128],
                                  rhs_t[i][:, t0:t0 + TS],
                                  start=False, stop=(ti == DT - 1)),
                         wait=w0, inc=s_dn)
                    if ti == DT - 1:
                        RDY[f"dn{bk}"] = s_dn.at()
                pend.append(omm)
            if hm >= 1:
                pend.append(mk_evac(hm - 1))
            if hm == HT - 1:
                pend.append(mk_evac(hm))

        def fin(b=b):
            for it in range(IT):
                FREE[f"hsub{it}"] = s_dn.at()
            FREE["hsub_arena"] = s_dn.at() + s_hs.at()
        pend.append(fin)

    BLEAD = NBB - 1
    GSEQ = [(c_, b_, n_) for c_ in range(NCH) for b_ in range(B)
            for _ti in range(DT) for n_ in range(N)]
    gbc = [0]

    def emit_bcast_global():
        k = gbc[0]
        gbc[0] += 1
        cg, bg, ng = GSEQ[k]
        pbc = k % NBB

        def bbf(e, bg=bg, ng=ng, pbc=pbc, cg=cg):
            a = bc_v[bg, ng, :, cg * SP:(cg + 1) * SP]
            a = a.unsqueeze(0).broadcast_to([128, GB, SP])
            return e.dma_start(
                out=bb_b[pbc][:, :].rearrange("p (s t) -> p s t", s=GB),
                in_=a)

        S.do("gpsimd", bbf, wait=RDY["bc_ag"] + fget(f"bb{pbc}"),
             inc=sem(f"s_bb{pbc}"), k=16)
        RDY[f"bbn{k}"] = sem(f"s_bb{pbc}").at()

        def cbf(e, bg=bg, ng=ng, pbc=pbc):
            a = bc_v[bg, N + ng, 0:nper, 0:SP]
            a = a.unsqueeze(0).broadcast_to([128, nper, SP])
            return e.dma_start(
                out=cb_b[pbc][:, :].rearrange("p (s t) -> p s t", s=nper),
                in_=a)

        S.do("gpsimd", cbf, wait=RDY["bc_ag"] + fget(f"cb{pbc}"),
             inc=sem(f"s_cb{pbc}"), k=16)
        RDY[f"cbn{k}"] = sem(f"s_cb{pbc}").at()

    def emit_scan_b(c, b):
        for ti in range(DT):
            i = b * DT + ti
            for n in range(N):
                kglob = ((c * B + b) * DT + ti) * N + n
                pbc = kglob % NBB
                while gbc[0] <= kglob + BLEAD and gbc[0] < len(GSEQ):
                    emit_bcast_global()
                RDY["bb"] = RDY[f"bbn{kglob}"]
                RDY["cb"] = RDY[f"cbn{kglob}"]
                q = it_scan[0] % 2
                qd = it_scan[0] % NDA
                it_scan[0] += 1
                S.do("scalar", lambda e, i=i, ti=ti, n=n, qd=qd: e.activation(
                    da_b[qd][:, :], dt_ch[i][:, :], AF.Exp,
                    scale=a_scr[:, ti * N + n:ti * N + n + 1]),
                    wait=rget(f"dt_ch{i}") + RDY["a_t"] + fget(f"da{qd}"),
                    inc=s_da)
                S.do("vector", lambda e, i=i, q=q, pbc=pbc: e.tensor_mul(
                    u_b[q][:, :], dx_ch[i][:, :], bb_b[pbc][:, :]),
                    wait=rget(f"dx_ch{i}") + RDY["bb"] + fget(f"u{q}"),
                    inc=s_u)
                FREE[f"bb{pbc}"] = s_u.at()
                if c == 0:
                    init = 0.0
                    winit = []
                else:
                    init = carry[i][:, n:n + 1]
                    winit = rget(f"carry{i}_{n}")
                S.do("vector", lambda e, q=q, qd=qd, init=init:
                     e.tensor_tensor_scan(
                         h_b[q][:, :], da_b[qd][:, :], u_b[q][:, :], init,
                         OP.mult, OP.add),
                     wait=s_da.at() + s_u.at() + fget(f"h{q}") + winit,
                     inc=s_h)
                FREE[f"da{qd}"] = s_h.at()
                FREE[f"u{q}"] = s_h.at()
                ch = s_h.at()
                S.do("gpsimd", lambda e, i=i, n=n, q=q: e.tensor_copy(
                    carry[i][:, n:n + 1], h_b[q][:, TC - 1:TC]),
                    wait=ch, inc=s_cp)
                RDY[f"carry{i}_{n}"] = s_cp.at()

                def zf(e, q=q, pbc=pbc):
                    if REP == 1:
                        return e.tensor_mul(z_b[q][:, :], h_b[q][:, :],
                                            cb_b[pbc][:, :])
                    hh = h_b[q][:, :].rearrange("p (r t) -> p r t", r=REP)
                    zz = z_b[q][:, :].rearrange("p (r t) -> p r t", r=REP)
                    cc2 = cb_b[pbc][:, :].unsqueeze(1).broadcast_to(
                        [128, REP, CHW])
                    return e.tensor_tensor(zz, hh, cc2, OP.mult)

                S.do("vector", zf,
                     wait=ch + RDY["cb"] + fget(f"z{q}"), inc=s_z)
                FREE[f"h{q}"] = s_z.at() + s_cp.at()
                FREE[f"cb{pbc}"] = s_z.at()
                for yh in range(YH):
                    w0 = s_z.at() + (fget(f"pbank{yh}") if n == 0 else [])
                    S.do("tensor", lambda e, q=q, yh=yh, n=n: e.matmul(
                        pb[yh][:, 0:YC], id_bf[:, :],
                        z_b[q][:, yh * YC:(yh + 1) * YC],
                        start=(n == 0), stop=(n == N - 1)),
                        wait=w0, inc=s_ymm)
                FREE[f"z{q}"] = s_ymm.at()
                left = (DT - ti) * N - (n + 1)
                rate = -(-len(pend) // max(1, left)) if left else len(pend)
                pop_pend(rate)
            for yh in range(YH):
                S.do("vector", lambda e, yh=yh: e.tensor_copy(
                    z_b[0][:, yh * YC:(yh + 1) * YC], pb[yh][:, 0:YC]),
                    wait=s_ymm.at() + fget("z0"), inc=s_yev)
                FREE[f"pbank{yh}"] = s_yev.at()
            cy = s_yev.at()
            S.do("vector", lambda e, i=i, ti=ti: e.scalar_tensor_tensor(
                rhs_t[i][:, :], xnd_t[ti][:, :], dpar_t[:, ti:ti + 1],
                z_b[0][:, :], OP.mult, OP.add),
                wait=cy + rget(f"xndt{ti}") + RDY["consts"] + fget(f"rhs{i}"),
                inc=s_rhs)
            FREE["z0"] = s_rhs.at()
            FREE[f"xndt{ti}"] = s_rhs.at()
            RDY[f"rhs{i}"] = s_rhs.at()
            FREE[f"dt_ch{i}"] = s_da.at()
            FREE[f"dx_ch{i}"] = s_u.at()

    def emit_xnd_load(c, b):
        for ti in range(DT):
            def f(e, c=c, b=b, ti=ti):
                a = xn_a2a_out.ap().rearrange("(j dd) t -> j dd t", j=NC)
                a = a.rearrange("(b s) dd t -> b s dd t", b=B)
                a = a[b, :, ti * 128:(ti + 1) * 128, c * SP:(c + 1) * SP]
                a = a.transpose([1, 0, 2])
                return e.dma_start(
                    out=xnd_t[ti][:, :].rearrange("p (s t) -> p s t", s=GB),
                    in_=a)
            sxn = sem(f"s_xnd{ti}")
            S.do("sync", f, wait=RDY["xn_a2a"] + fget(f"xndt{ti}"),
                 inc=sxn, k=16)
            RDY[f"xndt{ti}"] = sxn.at()

    def emit_rs(c):
        S.do("gpsimd", lambda e, c=c: e.collective_compute(
            "ReduceScatter", OP.add, replica_groups=[list(range(NC))],
            ins=[rs_in[c].ap().opt()], outs=[rs_out[c].ap().opt()]),
            wait=sem("s_rsw0").at() + sem("s_rsw1").at(),
            inc=sem(f"s_ccrs{c}"), k=1)
        RDY[f"rs{c}"] = sem(f"s_ccrs{c}").at()

    emit_chunk_loads(0)
    for c in range(NCH):
        for b in range(B):
            emit_xnd_load(c, b)
            queue_ffn_sub(c, b, 0)
            emit_scan_b(c, b)
            if c + 1 < NCH:
                emit_chunk_loads(c + 1, only_b=b)
            if b == 1 and c >= 1:
                emit_rs(c - 1)
            queue_down_sub(c, b, 0)
            for hf in range(1, NSUB):
                queue_ffn_sub(c, b, hf)
                queue_down_sub(c, b, hf)
    pop_pend(len(pend))
    emit_rs(NCH - 1)

    assert SP <= 256
    FREE["evslot0"] = fget("ev0")
    FREE["evslot1"] = fget("ev1")
    FREE["evslot2"] = fget("ev0")
    FREE["evslot3"] = fget("ev1")
    FREE["ffin0"] = fget("hsub_arena")
    FREE["ffin1"] = fget("hsub_arena")
    FREE["xf0"] = fget("hsub_arena")
    FREE["xf1"] = fget("hsub_arena")

    def emit_final(c):
        NG = D // 512
        for grp in range(NG):
            c_xf = []
            for tb in range(SP // 128):
                sx = sem(f"s_xf{tb}")
                S.do("sync", lambda e, c=c, grp=grp, tb=tb: e.dma_start(
                    out=xf_g[tb][:, :],
                    in_=x_own[c * SP + tb * 128:c * SP + (tb + 1) * 128,
                              grp * 512:(grp + 1) * 512]),
                    wait=fget(f"xf{tb}"), inc=sx, k=16)
                c_xf = c_xf + sx.at()

            def slot(j):
                return ev_b[j % 2][:, (j // 2) * SP:(j // 2 + 1) * SP]

            c_rb = []
            for j in range(4):
                dk = grp * 4 + j
                sj = sem(f"s_fev{j}")
                S.do("sync", lambda e, c=c, dk=dk, j=j: e.dma_start(
                    out=slot(j), in_=rs_out[c][dk * 128:(dk + 1) * 128, :]),
                    wait=rget(f"rs{c}") + fget(f"evslot{j}"), inc=sj, k=16)
                c_rb = c_rb + sj.at()
            for tb in range(SP // 128):
                bk = 6 + tb
                bank = pb[bk]
                for j in range(4):
                    src = slot(j)[:, tb * 128:(tb + 1) * 128]
                    S.do("tensor", lambda e, j=j, bank=bank, src=src:
                         e.transpose(bank[:, j * 128:(j + 1) * 128], src,
                                     id_f[:, :]),
                         wait=c_rb + fget(f"pbank{bk}"), inc=s_ftp)
                ct = s_ftp.at()
                S.do("vector", lambda e, tb=tb, bank=bank: e.tensor_add(
                    ffin[tb][:, :], bank[:, 0:512], xf_g[tb][:, :]),
                    wait=ct + c_xf + fget(f"ffin{tb}"), inc=s_fa)
                FREE[f"pbank{bk}"] = s_fa.at()
                so = sem(f"s_out{tb}")
                S.do("sync", lambda e, c=c, grp=grp, tb=tb: e.dma_start(
                    out=y_out[c, tb * 128:(tb + 1) * 128,
                              grp * 512:(grp + 1) * 512],
                    in_=ffin[tb][:, :]),
                    wait=s_fa.at(), inc=so, k=16)
                FREE[f"ffin{tb}"] = so.at()
                FREE[f"xf{tb}"] = so.at()
            for j in range(4):
                FREE[f"evslot{j}"] = s_ftp.at()

    for c in range(NCH):
        emit_final(c)

    end_w = sem("s_out0").at() + sem("s_out1").at() + s_cp.at()
    for c in range(NCH):
        end_w = end_w + sem(f"s_ccrs{c}").at()
    S.do("sync", None, wait=end_w)
    S.do("gpsimd", None, wait=sem("s_out0").at() + sem("s_out1").at())

    with allow_ici_all_to_all():
        S.emit(nc)
    return nc


def shard_inputs(cfg, inp):
    D, N, R = cfg["D"], cfg["N"], cfg["R"]
    NC, NCH, TC, SP = cfg["NC"], cfg["NCH"], cfg["TC"], cfg["SP"]
    DOWN, DT, KT, IOWN, IT = (cfg["DOWN"], cfg["DT"], cfg["KT"], cfg["IOWN"],
                              cfg["IT"])
    GB = cfg["GB"]

    x = np.asarray(inp["L0_attn_x"], np.float32)
    gw = np.asarray(inp["L0_ffn_gate_w"], np.int32)
    uw = np.asarray(inp["L0_ffn_up_w"], np.int32)
    dw = np.asarray(inp["L0_ffn_down_w"], np.int32)
    gsc = np.asarray(inp["L0_ffn_gate_scale"], np.float32)
    usc = np.asarray(inp["L0_ffn_up_scale"], np.float32)
    dsc = np.asarray(inp["L0_ffn_down_scale"], np.float32)
    op = np.asarray(inp["L0_attn_out_proj_w"], np.float32)
    xp = np.asarray(inp["L0_attn_x_proj_w"], np.float32)
    dp = np.asarray(inp["L0_attn_dt_proj_w"], np.float32)
    dpb = np.asarray(inp["L0_attn_dt_proj_b"], np.float32)
    nw = np.asarray(inp["L0_attn_norm_w"], np.float32)
    nb = np.asarray(inp["L0_attn_norm_b"], np.float32)
    dpar = np.asarray(inp["L0_attn_D_param"], np.float32)
    alog = np.asarray(inp["L0_attn_A_log"], np.float32)

    def cols(v, nt):
        return np.ascontiguousarray(v.reshape(nt, 128).T)

    ident_bf = np.eye(128, dtype=BFNP)
    ident_f = np.eye(128, dtype=np.float32)
    xpT = np.ascontiguousarray(xp.T)
    dpT = np.ascontiguousarray(dp.T)
    dpb_c, nw_c, nb_c = cols(dpb, KT), cols(nw, KT), cols(nb, KT)

    maps = []
    for k in range(NC):
        b, s = k // GB, k % GB
        xb = x[b]
        stripes = [xb[c * TC + s * SP:c * TC + (s + 1) * SP] for c in range(NCH)]
        Dk = slice(k * DOWN, (k + 1) * DOWN)
        Ik = slice(k * IOWN, (k + 1) * IOWN)
        al = alog[Dk]
        al_t = np.zeros((128, DT * N), np.float32)
        for h in range(DT):
            al_t[:, h * N:(h + 1) * N] = al[h * 128:(h + 1) * 128, :]
        maps.append({
            "x_own": np.ascontiguousarray(np.concatenate(stripes, axis=0)),
            "wgT": np.ascontiguousarray(gw[Ik].T),
            "wuT": np.ascontiguousarray(uw[Ik].T),
            "wdT": np.ascontiguousarray(dw[:, Ik].T),
            "gs": cols(gsc[Ik], IT),
            "us": cols(usc[Ik], IT),
            "ds": np.ascontiguousarray(dsc.reshape(1, D)),
            "opT": np.ascontiguousarray(op[:, Dk].T),
            "xpT": xpT,
            "dpT": dpT,
            "dpb": dpb_c,
            "nw": nw_c,
            "nb": nb_c,
            "dpar": cols(dpar[Dk], DT),
            "alog": al_t,
            "ident_bf": ident_bf,
            "ident_f32": ident_f,
            "fconst": np.tile(np.array(
                [[1e-5, -0.5, -1.0, 1.0, -128.0, 0.0, 2.0, 0.5]], np.float32),
                (128, 1)),
        })
    return maps


def assemble_output(cfg, results):
    B, L, D = cfg["B"], cfg["L"], cfg["D"]
    NC, NCH, TC, SP, GB = cfg["NC"], cfg["NCH"], cfg["TC"], cfg["SP"], cfg["GB"]
    out = np.empty((B, L, D), np.float32)
    for k in range(NC):
        b, s = k // GB, k % GB
        yo = np.asarray(results[k]["y_out"], np.float32).reshape(NCH, SP, D)
        for c in range(NCH):
            out[b, c * TC + s * SP:c * TC + (s + 1) * SP] = yo[c]
    return out


def kernel(**inputs):
    cfg = CFG
    nc = build_program(cfg)
    in_maps = shard_inputs(cfg, inputs)
    res = run_bass_kernel_spmd(nc, in_maps, core_ids=list(range(cfg["NC"])))
    return assemble_output(cfg, res.results)
